# version 01
# baseline (speedup 1.0000x reference)
import numpy as np
import concourse.bass as bass
import concourse.mybir as mybir
from concourse.bass_utils import run_bass_kernel_spmd
from contextlib import ExitStack

F32 = mybir.dt.float32
BF16 = mybir.dt.bfloat16
U8 = mybir.dt.uint8
AF = mybir.ActivationFunctionType
ALU = mybir.AluOpType
AX = mybir.AxisListType

P = 128
D = 4096
KC = D // P
CONVO = 3072
NCO = CONVO // P
W_ = 1024
HEC = W_ // P
H, DH = 16, 64
DFF = 11008
FC = DFF // P
MT = 512
NTB = MT // P
NT = 2
TPC = NT * MT
NCORES = 8
SEQ = 2048
NEG = -1.0e9
EPS = 1e-5
INV_D = 1.0 / 4096.0

ENGINES = ("tensor", "vector", "scalar", "gpsimd", "sync")

def BK(i, w=512):
    return slice(i * 512, i * 512 + w)


class Sched:
    def __init__(self, nc, st):
        self.nc = nc
        self.st = st
        self.q = {e: [] for e in ENGINES}
        self.sems = {}
        self.cnt = {}
        self.last_wait = {e: {} for e in ENGINES}

    def new_sem(self, name):
        h = self.st.enter_context(self.nc.semaphore(name))
        self.sems[name] = h
        self.cnt[name] = 0
        return name

    @staticmethod
    def _flat(waits):
        out = []
        for w in waits:
            if isinstance(w, tuple):
                out.append(w)
            else:
                out.extend(w)
        return out

    def emit(self, eng, fn, waits=(), incs=()):
        w = []
        for sname, val in self._flat(waits):
            if val <= 0:
                continue
            if self.last_wait[eng].get(sname, -1) >= val:
                continue
            self.last_wait[eng][sname] = val
            w.append((self.sems[sname], val))
        ii = []
        for sname, k in incs:
            self.cnt[sname] += k
            ii.append((self.sems[sname], k))
        self.q[eng].append((fn, w, ii))

    def op(self, eng, fn, waits=(), inc=False):
        sname = "e_" + eng
        self.emit(eng, fn, waits=waits, incs=([(sname, 1)] if inc else ()))
        return (sname, self.cnt[sname])

    def tok(self, eng):
        return ("e_" + eng, self.cnt["e_" + eng])


class DmaQ:
    def __init__(self, S, eng, name):
        self.S = S
        self.eng = eng
        self.sem = [S.new_sem(name + "0"), S.new_sem(name + "1")]
        self.i = 0

    def dma(self, fn, waits=()):
        par = self.i % 2
        self.i += 1
        sname = self.sem[par]
        w = list(waits) + [(sname, self.S.cnt[sname])]
        self.S.emit(self.eng, fn, waits=w, incs=[(sname, 16)])
        return [(s, self.S.cnt[s]) for s in self.sem]

    def final_waits(self):
        return [(s, self.S.cnt[s]) for s in self.sem]


def build_core(stop=None):
    nc = bass.Bass("TRN2", target_bir_lowering=False)
    dr = {}

    def din(name, shape, dtype=F32):
        dr[name] = nc.dram_tensor(name, list(shape), dtype, kind="ExternalInput").ap()
        return dr[name]

    x_in = din("x_in", (TPC + 2 * P, D))
    convw = din("convw", (3, D, CONVO))
    convb = din("convb", (CONVO, 1))
    wq_in = din("wq_in", (W_, W_))
    wk_in = din("wk_in", (W_, W_))
    wv_in = din("wv_in", (W_, W_))
    wo_in = din("wo_in", (W_, W_))
    bq_in = din("bq_in", (W_, 1))
    bk_in = din("bk_in", (W_, 1))
    bv_in = din("bv_in", (1, W_))
    bo_in = din("bo_in", (W_, 1))
    projw = din("projw", (W_, D))
    projb = din("projb", (P, KC))
    gb_in = din("gb_in", (FC, P, KC * 64), U8)
    ub_in = din("ub_in", (FC, P, KC * 64), U8)
    db_in = din("db_in", (8, FC, P, 256), U8)
    gs_in = din("gs_in", (KC, DFF))
    us_in = din("us_in", (KC, DFF))
    ds_in = din("ds_in", (FC, D))
    gz_in = din("gz_in", (KC, DFF // 2), U8)
    uz_in = din("uz_in", (KC, DFF // 2), U8)
    dz_in = din("dz_in", (FC, D // 2), U8)

    out_d = nc.dram_tensor("out_d", [D, TPC], F32, kind="ExternalOutput").ap()
    y1_sp = nc.dram_tensor("y1_spill", [NT, D, MT], F32,
                           kind=("ExternalOutput" if stop == "y1" else "Internal")).ap()
    at_sp = nc.dram_tensor("attn_spill", [D, MT], F32,
                           kind=("ExternalOutput" if stop in ("attn_sp", "ffn_sp")
                                 else "Internal")).ap()
    zsg_d = nc.dram_tensor("zsg_d", [KC, DFF], BF16, kind="Internal").ap()
    zsu_d = nc.dram_tensor("zsu_d", [KC, DFF], BF16, kind="Internal").ap()
    zsd_d = nc.dram_tensor("zsd_d", [FC, D], BF16, kind="Internal").ap()

    dbg = {}
    st = ExitStack()
    with st:
        AR1 = st.enter_context(nc.sbuf_tensor("AR1", [P, 66560], U8))
        AR2 = st.enter_context(nc.sbuf_tensor("AR2", [P, 90112], U8))
        AR3 = st.enter_context(nc.sbuf_tensor("AR3", [P, 33536], U8))
        identf = st.enter_context(nc.sbuf_tensor("identf", [P, P], F32))
        identb = st.enter_context(nc.sbuf_tensor("identb", [P, P], BF16))
        onesr = st.enter_context(nc.sbuf_tensor("onesr", [1, P], F32))
        onesr_b = st.enter_context(nc.sbuf_tensor("onesr_b", [1, P], BF16))
        onescol_b = st.enter_context(nc.sbuf_tensor("onescol_b", [P, 1], BF16))
        c32 = st.enter_context(nc.sbuf_tensor("c32", [P, 64], BF16))
        c86 = st.enter_context(nc.sbuf_tensor("c86", [P, 172], BF16))
        convbuf = st.enter_context(nc.sbuf_tensor("convbuf", [P, 2 * 1024], BF16))
        biases = st.enter_context(nc.sbuf_tensor("biases", [P, 80], F32))
        lnsc = st.enter_context(nc.sbuf_tensor("lnsc", [P, 16], F32))
        attn_mx = st.enter_context(nc.sbuf_tensor("attn_mx", [P, 3 * H], F32))
        xg_sb = st.enter_context(nc.sbuf_tensor("xg_sb", [KC, MT], BF16))
        hg_sb = st.enter_context(nc.sbuf_tensor("hg_sb", [FC, MT], BF16))
        r512 = st.enter_context(nc.sbuf_tensor("r512", [1, 4 * MT], F32))
        constt = st.enter_context(nc.sbuf_tensor("constt", [P, 2], F32))
        PSA_t = st.enter_context(nc.psum_tensor("PSA", [P, 4096], F32))
        PSA = PSA_t[:]
        PSAb = PSA_t[:].bitcast(BF16)

        S = Sched(nc, st)
        for e in ENGINES:
            S.new_sem("e_" + e)
        qX = DmaQ(S, "sync", "dX")
        qY = DmaQ(S, "sync", "dY")
        qB = DmaQ(S, "sync", "dB")
        qW = DmaQ(S, "gpsimd", "dW")

        def carve(arena, off_bytes, pcount, fcount, dtype):
            esz = {F32: 4, BF16: 2, U8: 1}[dtype]
            ap = arena[:].bitcast(dtype)
            assert off_bytes % esz == 0, (off_bytes, dtype)
            o = off_bytes // esz
            assert (o + fcount) * esz <= arena.shape[1], \
                f"carve overflow: {off_bytes}+{fcount * esz} > {arena.shape[1]}"
            return ap[0:pcount, o:o + fcount]

        x_blk = [carve(AR1, b * 16384, P, D, F32) for b in range(NTB)]
        gu_byte = [carve(AR1, i * 2048, P, 2048, U8) for i in range(2)]
        gu_nib = [carve(AR1, 4096 + i * 4096, P, 4096, U8) for i in range(2)]
        gu_w = [carve(AR1, 12288 + i * 8192, P, 4096, BF16) for i in range(2)]
        sel32 = carve(AR1, 28672, KC, KC * P, BF16)
        ssl = [carve(AR1, 36864 + i * 256, KC, P, BF16) for i in range(2)]
        zsl = [carve(AR1, 37376 + i * 256, KC, P, BF16) for i in range(2)]
        silu_sb = carve(AR1, 38912, P, MT, F32)
        zb_scr = carve(AR1, 0, KC, DFF // 2, U8)
        zn_scr = carve(AR1, 5504, KC, DFF, U8)
        zs_scr = carve(AR1, 16512, KC, DFF // 4, BF16)
        zb86_scr = carve(AR1, 0, FC, D // 2, U8)
        zn86_scr = carve(AR1, 5504, FC, D, U8)
        zs86_scr = carve(AR1, 16512, FC, D // 2, BF16)
        dn_byte = [carve(AR1, i * 1024, P, 1024, U8) for i in range(2)]
        dn_nib = [carve(AR1, 2048 + i * 512, P, 512, U8) for i in range(2)]
        dn_w = [carve(AR1, 3072 + i * 1024, P, 512, BF16) for i in range(2)]
        dsl = [carve(AR1, 5120 + i * 1024, 1, MT, BF16) for i in range(2)]
        nzd = [carve(AR1, 7168 + i * 256, FC, P, BF16) for i in range(2)]
        st_sq = carve(AR1, 8192, P, MT, BF16)
        st_cp = carve(AR1, 9216, P, MT, BF16)
        ffn_stage = [carve(AR1, 10240 + i * 2048, P, MT, F32) for i in range(2)]

        ACTW = 516
        actT = carve(AR3, 0, P, KC * ACTW, BF16)

        h_fm = [carve(AR2, k * 1024, P, MT, BF16) for k in range(FC)]
        qkv = [carve(AR2, i * 1024, P, MT, BF16) for i in range(NCO)]
        qh_s = [carve(AR2, 24576 + i * 1024, P, MT, BF16) for i in range(HEC)]
        kh_s = [carve(AR2, 32768 + i * 1024, P, MT, BF16) for i in range(HEC)]
        vh_tm = [carve(AR2, 40960 + tb * 2048, P, W_, BF16) for tb in range(NTB)]
        o_fm = [carve(AR2, 49152 + i * 1024, P, MT, BF16) for i in range(HEC)]
        o_fm_all = carve(AR2, 49152, P, HEC * MT, BF16)
        wmat = carve(AR2, 57344, P, HEC * 1024, BF16)
        mask16 = carve(AR2, 73728, P, H * P, F32)
        bv_bc = carve(AR2, 81920, P, W_, F32)
        bvrow = carve(AR2, 86016, 1, W_, F32)
        xh_blk = carve(AR2, 0, P, D, F32)
        sqjunk = carve(AR2, 16384, P, D, BF16)
        xln_scr = carve(AR2, 24576, P, D, BF16)
        attn_sc_v = carve(AR2, 0, P, H * P, F32)
        attn_e_v = carve(AR2, 8192, P, H * P, F32)
        attn_p_v = carve(AR2, 16384, P, H * P, BF16)
        attn_pT_v = carve(AR2, 20480, P, H * P, BF16)
        qh_od = [carve(AR2, 81920 + i * 1024, 64, MT, BF16) for i in range(HEC)]
        kh_od = [carve(AR2, 57344 + i * 1024, 64, MT, BF16) for i in range(HEC)]
        pjw_b = [carve(AR2, i * 2048, P, HEC * P, BF16) for i in range(2)]
        at_st = [carve(AR2, 4096 + i * 2048, P, MT, F32) for i in range(2)]
        xfm_st = carve(AR2, 8192, P, MT, F32)
        y1_st = carve(AR2, 10240, P, MT, F32)
        y1_ld = [carve(AR2, 12288 + i * 2048, P, MT, F32) for i in range(2)]
        p7_sq = carve(AR2, 16384, P, MT, BF16)
        p7_cp = carve(AR2, 17408, P, MT, BF16)
        ffn_ld = [carve(AR2, 18432 + i * 2048, P, MT, F32) for i in range(2)]
        y2_st = [carve(AR2, 22528 + i * 2048, P, MT, F32) for i in range(2)]

        g = "gpsimd"

        export_chain = [None]

        def emit_export(name, ap, pcount, fcount, is_bf16, waits):
            tpn = "dbg_" + name
            tp = nc.dram_tensor(tpn, [pcount, fcount], F32, kind="ExternalOutput").ap()
            if not is_bf16:
                tok = qX.dma(lambda e: e.dma_start(out=tp, in_=ap), waits=waits)
                dbg[name] = tpn
                export_chain[0] = tok
                return
            step = 2048
            tmp = carve(AR1, 40960, P, step, F32)
            last = export_chain[0]
            for o in range(0, fcount, step):
                wdt = min(step, fcount - o)
                ct = S.op("scalar", lambda e, o=o, wdt=wdt, ap=ap: e.activation(
                    out=tmp[0:pcount, 0:wdt], in_=ap[0:pcount, o:o + wdt],
                    func=AF.Copy), waits=list(waits) + ([last] if last else []),
                    inc=True)
                last = qX.dma(lambda e, o=o, wdt=wdt: e.dma_start(
                    out=tp[0:pcount, o:o + wdt], in_=tmp[0:pcount, 0:wdt]),
                    waits=[ct])
            dbg[name] = tpn
            export_chain[0] = last

        for ident, dt in ((identf, F32), (identb, BF16)):
            S.op(g, lambda e, ident=ident: e.memset(ident[:], 0.0))
            S.op(g, lambda e, ident=ident: e.affine_select(
                out=ident[:], in_=ident[:], compare_op=ALU.not_equal, fill=1.0,
                base=0, pattern=[[-1, P]], channel_multiplier=1))
        S.op(g, lambda e: e.memset(onesr[:], 1.0))
        S.op(g, lambda e: e.memset(onesr_b[:], 1.0))
        S.op(g, lambda e: e.memset(onescol_b[:], 1.0))
        S.op(g, lambda e: e.memset(constt[:, 0:1], EPS))
        S.op(g, lambda e: e.memset(constt[:, 1:2], 0.125))
        S.op(g, lambda e: e.memset(c32[:], 0.0))
        S.op(g, lambda e: e.memset(c32[:, 32:33], 1.0))
        S.op(g, lambda e: e.memset(c86[:], 0.0))
        kconst = S.op(g, lambda e: e.memset(c86[:, 86:87], 1.0), inc=True)

        last_b = None
        for ncx in range(NCO):
            last_b = qW.dma(lambda e, ncx=ncx: e.dma_start(
                out=biases[:, ncx:ncx + 1], in_=convb[ncx * P:(ncx + 1) * P, :]))
        for i, src in ((0, bq_in), (1, bk_in), (2, bo_in)):
            for hc in range(HEC):
                last_b = qW.dma(lambda e, i=i, hc=hc, src=src: e.dma_start(
                    out=biases[:, 24 + i * 8 + hc:24 + i * 8 + hc + 1],
                    in_=src[hc * P:(hc + 1) * P, :]))
        last_b = qW.dma(lambda e: e.dma_start(
            out=biases[:, 48:80], in_=projb))
        bias_tok = S.op("vector", lambda e: e.tensor_scalar(
            out=biases[:, 24:32], in0=biases[:, 24:32], scalar1=0.125, scalar2=None,
            op0=ALU.mult), waits=[last_b], inc=True)

        def ext4(eng, byte_ap, nib_ap, Wrd, waits, inc_last=True):
            bq = byte_ap.rearrange("p (w t f) -> p w t f", w=Wrd, t=2, f=2)
            nv = nib_ap.rearrange("p (w q t) -> p w q t", w=Wrd, q=4, t=2)
            ops = [(ALU.bitwise_and, 15, 0, 0),
                   (ALU.bitwise_and, 15, 1, 2),
                   (ALU.logical_shift_right, 4, 0, 1),
                   (ALU.logical_shift_right, 4, 1, 3)]
            tok = None
            for idx, (alu, scl, fsel, qb) in enumerate(ops):
                iv = bq[:, :, :, fsel:fsel + 1]
                ov = nv[:, :, qb:qb + 1, :]
                tok = S.op(eng, lambda e, iv=iv, ov=ov, alu=alu, scl=scl:
                           e.tensor_single_scalar(out=ov, in_=iv, scalar=scl, op=alu),
                           waits=waits, inc=(inc_last and idx == 3))
            return tok

        for (zin, sin, zout) in ((gz_in, gs_in, zsg_d), (uz_in, us_in, zsu_d)):
            zb_tok = qX.dma(lambda e, zin=zin: e.dma_start(out=zb_scr, in_=zin),
                            waits=[S.tok("vector")])
            etok = ext4("vector", zb_scr, zn_scr, DFF // 8, [zb_tok, kconst])
            for q4 in range(4):
                w0, w1 = q4 * (DFF // 4), (q4 + 1) * (DFF // 4)
                s_tok = qW.dma(lambda e, sin=sin, w0=w0, w1=w1: e.dma_start(
                    out=zs_scr, in_=sin[:, w0:w1]), waits=[S.tok("vector")])
                m_tok = S.op("vector", lambda e, w0=w0, w1=w1: e.tensor_tensor(
                    out=zs_scr, in0=zn_scr[:, w0:w1], in1=zs_scr, op=ALU.mult),
                    waits=[etok, s_tok], inc=True)
                m_tok = S.op("vector", lambda e: e.tensor_scalar(
                    out=zs_scr, in0=zs_scr, scalar1=-1.0, scalar2=None, op0=ALU.mult),
                    waits=[m_tok], inc=True)
                qY.dma(lambda e, zout=zout, w0=w0, w1=w1: e.dma_start(
                    out=zout[:, w0:w1], in_=zs_scr), waits=[m_tok])
        zb_tok = qX.dma(lambda e: e.dma_start(out=zb86_scr, in_=dz_in),
                        waits=[S.tok("vector")])
        etok = ext4("vector", zb86_scr, zn86_scr, D // 8, [zb_tok, kconst])
        for q2 in range(2):
            w0, w1 = q2 * (D // 2), (q2 + 1) * (D // 2)
            s_tok = qW.dma(lambda e, w0=w0, w1=w1: e.dma_start(
                out=zs86_scr, in_=ds_in[:, w0:w1]), waits=[S.tok("vector")])
            m_tok = S.op("vector", lambda e, w0=w0, w1=w1: e.tensor_tensor(
                out=zs86_scr, in0=zn86_scr[:, w0:w1], in1=zs86_scr, op=ALU.mult),
                waits=[etok, s_tok], inc=True)
            m_tok = S.op("vector", lambda e: e.tensor_scalar(
                out=zs86_scr, in0=zs86_scr, scalar1=-1.0, scalar2=None, op0=ALU.mult),
                waits=[m_tok], inc=True)
            qY.dma(lambda e, w0=w0, w1=w1: e.dma_start(
                out=zsd_d[:, w0:w1], in_=zs86_scr), waits=[m_tok])
        zs_ready = qY.final_waits()

        def ln_stats_tm(T_ap, waits):
            v, s = "vector", "scalar"
            t1 = S.op(v, lambda e: e.reduce_sum(lnsc[:, 2:3], T_ap, axis=AX.X),
                      waits=waits, inc=True)
            t2 = S.op(s, lambda e: e.activation(
                out=sqjunk, in_=T_ap, func=AF.Square, accum_out=lnsc[:, 3:4]),
                waits=waits, inc=True)
            t1 = S.op(v, lambda e: e.tensor_scalar(
                out=lnsc[:, 0:1], in0=lnsc[:, 2:3], scalar1=INV_D, scalar2=None,
                op0=ALU.mult), waits=[t1, t2], inc=True)
            t1 = S.op(v, lambda e: e.tensor_tensor(
                out=lnsc[:, 4:5], in0=lnsc[:, 0:1], in1=lnsc[:, 0:1], op=ALU.mult),
                waits=[t1], inc=True)
            t1 = S.op(v, lambda e: e.scalar_tensor_tensor(
                out=lnsc[:, 5:6], in0=lnsc[:, 3:4], scalar=INV_D, in1=lnsc[:, 4:5],
                op0=ALU.mult, op1=ALU.subtract), waits=[t1], inc=True)
            t2 = S.op(s, lambda e: e.activation(
                out=lnsc[:, 6:7], in_=lnsc[:, 5:6], func=AF.Sqrt, bias=constt[:, 0:1]),
                waits=[t1], inc=True)
            t1 = S.op(v, lambda e: e.reciprocal(lnsc[:, 1:2], lnsc[:, 6:7]),
                      waits=[t2], inc=True)
            return t1

        def transpose_block(src_bf_ap, dst_col_fn, wait_src):
            evac_toks = []
            for kcq in range(KC // 4):
                bank = kcq % 2
                pso = PSAb[:, bank * 1024: bank * 1024 + 1024]
                for j in range(4):
                    kc = kcq * 4 + j
                    w = [wait_src]
                    if kcq >= 2:
                        w.append(evac_toks[kcq - 2])
                    S.op("tensor", lambda e, kc=kc, j=j, pso=pso: e.transpose(
                        pso[:, j * P:(j + 1) * P],
                        src_bf_ap[:, kc * P:(kc + 1) * P], identb[:]),
                        waits=w, inc=(j == 3))
                tt = S.tok("tensor")
                for j in range(4):
                    kc = kcq * 4 + j
                    dap, width, soff = dst_col_fn(kc)
                    S.op("scalar", lambda e, dap=dap, width=width, soff=soff, j=j,
                         pso=pso: e.activation(
                             out=dap, in_=pso[:, j * P + soff: j * P + soff + width],
                             func=AF.Copy), waits=[tt], inc=(j == 3))
                evac_toks.append(S.tok("scalar"))
            return evac_toks[-1]

        def ln_row_scalars(sum_ap, sq_ap, rb_slice, mb_slice, waits):
            v, s = "vector", "scalar"
            t1 = S.op(v, lambda e: e.tensor_scalar(
                out=r512[:, 0:512], in0=sum_ap, scalar1=INV_D, scalar2=None,
                op0=ALU.mult), waits=waits, inc=True)
            t1 = S.op(v, lambda e: e.tensor_tensor(
                out=r512[:, 1536:2048], in0=r512[:, 0:512], in1=r512[:, 0:512],
                op=ALU.mult), waits=[t1], inc=True)
            t1 = S.op(v, lambda e: e.scalar_tensor_tensor(
                out=r512[:, 1536:2048], in0=sq_ap, scalar=INV_D,
                in1=r512[:, 1536:2048], op0=ALU.mult, op1=ALU.subtract),
                waits=[t1], inc=True)
            t2 = S.op(s, lambda e: e.activation(
                out=r512[:, 1536:2048], in_=r512[:, 1536:2048], func=AF.Sqrt,
                bias=constt[0:1, 0:1]), waits=[t1], inc=True)
            t1 = S.op(v, lambda e: e.reciprocal(
                r512[:, 512:1024], r512[:, 1536:2048]), waits=[t2], inc=True)
            t1 = S.op(v, lambda e: e.tensor_tensor(
                out=r512[:, 1024:1536], in0=r512[:, 0:512], in1=r512[:, 512:1024],
                op=ALU.mult), waits=[t1], inc=True)
            t1 = S.op(v, lambda e: e.tensor_scalar(
                out=r512[:, 1024:1536], in0=r512[:, 1024:1536], scalar1=-1.0,
                scalar2=None, op0=ALU.mult), waits=[t1], inc=True)
            S.op("tensor", lambda e, rb=rb_slice: e.matmul(
                PSA[:, rb], onesr[:], r512[:, 512:1024], start=True, stop=True),
                waits=[t1])
            S.op("tensor", lambda e, mb=mb_slice: e.matmul(
                PSA[:, mb], onesr[:], r512[:, 1024:1536], start=True, stop=True),
                waits=[t1], inc=True)
            return S.tok("tensor")

        stopped = False
        for t in range(NT):
            if stopped:
                break
            tok0 = t * MT
            xrow0 = P + tok0
            sfx = "" if t == 0 else "@1"

            x_toks = []
            for b in range(NTB):
                x_toks.append(qX.dma(lambda e, b=b, xrow0=xrow0: e.dma_start(
                    out=x_blk[b], in_=x_in[xrow0 + b * P: xrow0 + (b + 1) * P, :]),
                    waits=zs_ready + [S.tok("vector"), S.tok("scalar"),
                                      S.tok("tensor")]))
            for b in range(NTB):
                stok = ln_stats_tm(x_blk[b], [x_toks[b], bias_tok, kconst])
                atok = S.op("vector", lambda e, b=b: e.tensor_scalar(
                    out=xln_scr, in0=x_blk[b], scalar1=lnsc[:, 0:1],
                    scalar2=lnsc[:, 1:2], op0=ALU.subtract, op1=ALU.mult),
                    waits=[stok], inc=True)
                tr_tok = transpose_block(
                    xln_scr,
                    lambda kc, b=b: (
                        actT[:, kc * ACTW + 1 + b * P: kc * ACTW + 1 + (b + 1) * P],
                        P, 0), atok)
            for side in range(2):
                if side == 0:
                    r0, r1 = xrow0 - P, xrow0
                    dstf = lambda kc: (actT[:, kc * ACTW: kc * ACTW + 1], 1, P - 1)
                else:
                    r0, r1 = xrow0 + MT, xrow0 + MT + P
                    dstf = lambda kc: (actT[:, kc * ACTW + 513: kc * ACTW + 514], 1, 0)
                htok = qX.dma(lambda e, r0=r0, r1=r1: e.dma_start(
                    out=xh_blk, in_=x_in[r0:r1, :]),
                    waits=[S.tok("scalar"), S.tok("vector")])
                stok = ln_stats_tm(xh_blk, [htok])
                atok = S.op("vector", lambda e: e.tensor_scalar(
                    out=xln_scr, in0=xh_blk, scalar1=lnsc[:, 0:1],
                    scalar2=lnsc[:, 1:2], op0=ALU.subtract, op1=ALU.mult),
                    waits=[stok], inc=True)
                tr_tok = transpose_block(xln_scr, dstf, atok)
            tr_all = tr_tok
            if stop == "actT" + sfx:
                emit_export("actT", actT, P, KC * ACTW, True, [tr_all])
                stopped = True
                break

            order = [(ncg, kc, w) for ncg in range(3) for kc in range(KC)
                     for w in range(3)]
            wtoks = {}
            mm_toks = {}

            def conv_dma(gi):
                ncg, kc, w = order[gi]
                buf = convbuf[:, (gi % 2) * 1024:(gi % 2) * 1024 + 1024]
                wwaits = [tr_all]
                if gi >= 2:
                    wwaits.append(mm_toks[gi - 2])
                wtoks[gi] = qW.dma(lambda e, kc=kc, w=w, buf=buf, ncg=ncg:
                                   e.dma_start(
                    out=buf, in_=convw[w, kc * P:(kc + 1) * P,
                                       ncg * 1024:(ncg + 1) * 1024]), waits=wwaits)

            conv_dma(0)
            conv_dma(1)
            qkv_tok = None
            for gi, (ncg, kc, w) in enumerate(order):
                buf = convbuf[:, (gi % 2) * 1024:(gi % 2) * 1024 + 1024]
                i_in_ncg = gi - ncg * 96
                ev_wait = S.tok("scalar")
                for nc8 in range(8):
                    S.op("tensor", lambda e, buf=buf, nc8=nc8, kc=kc, w=w,
                         i=i_in_ncg: e.matmul(
                             PSA[:, nc8 * 512:(nc8 + 1) * 512],
                             buf[:, nc8 * P:(nc8 + 1) * P],
                             actT[:, kc * ACTW + w: kc * ACTW + w + MT],
                             start=(i == 0), stop=(i == 95)),
                         waits=[wtoks[gi], ev_wait], inc=(nc8 == 7))
                mm_toks[gi] = S.tok("tensor")
                if gi + 2 < len(order):
                    conv_dma(gi + 2)
                if i_in_ncg == 95:
                    for nc8 in range(8):
                        ncx = ncg * 8 + nc8
                        qkv_tok = S.op("scalar", lambda e, nc8=nc8, ncx=ncx:
                                       e.activation(
                            out=qkv[ncx], in_=PSA[:, nc8 * 512:(nc8 + 1) * 512],
                            func=AF.Identity, bias=biases[:, ncx:ncx + 1]),
                            waits=[mm_toks[gi]], inc=True)
            qkv_done = qkv_tok
            if stop == "qkv" + sfx:
                emit_export("qkv", carve(AR2, 0, P, NCO * MT, BF16), P, NCO * MT,
                            True, [qkv_done])
                stopped = True
                break

            def load_wmat(src, waits):
                tok = None
                for c in range(HEC):
                    tok = qW.dma(lambda e, src=src, c=c: e.dma_start(
                        out=wmat[:, c * 1024:(c + 1) * 1024],
                        in_=src[c * P:(c + 1) * P, :]),
                        waits=(waits if c == 0 else ()))
                return tok

            def proj_heads(wtok, qkv_off, outbuf, bias_base, scale):
                done = []
                for hc in range(HEC):
                    bank = hc % 2
                    for kc in range(HEC):
                        S.op("tensor", lambda e, hc=hc, kc=kc, bank=bank: e.matmul(
                            PSA[:, bank * 512:(bank + 1) * 512],
                            wmat[:, kc * 1024 + hc * P: kc * 1024 + (hc + 1) * P],
                            qkv[qkv_off + kc], start=(kc == 0), stop=(kc == HEC - 1)),
                            waits=[wtok] + ([done[hc - 2]] if hc >= 2 else []),
                            inc=(kc == HEC - 1))
                    mt_ = S.tok("tensor")
                    done.append(S.op("scalar", lambda e, hc=hc, bank=bank:
                                     e.activation(
                        out=outbuf[hc], in_=PSA[:, bank * 512:(bank + 1) * 512],
                        func=AF.Identity, bias=biases[:, bias_base + hc:bias_base + hc + 1],
                        scale=scale), waits=[mt_], inc=True))
                return done[-1]

            wq_tok = load_wmat(wq_in, [qkv_done])
            qh_tok = proj_heads(wq_tok, 0, qh_s, 24, constt[:, 1:2])
            wk_tok = load_wmat(wk_in, [qh_tok])
            kh_tok = proj_heads(wk_tok, 8, kh_s, 32, 1.0)
            wv_tok = load_wmat(wv_in, [kh_tok])
            bv_tok = qW.dma(lambda e: e.dma_start(out=bvrow, in_=bv_in),
                            waits=[kh_tok])
            for i2 in range(2):
                S.op("tensor", lambda e, i2=i2: e.matmul(
                    PSA[:, 1024 + i2 * 512: 1024 + (i2 + 1) * 512],
                    onesr[:], bvrow[:, i2 * 512:(i2 + 1) * 512],
                    start=True, stop=True), waits=[bv_tok, kh_tok], inc=(i2 == 1))
            bvb_t = S.tok("tensor")
            bv_ev = S.op("vector", lambda e: e.tensor_copy(
                out=bv_bc, in_=PSA[:, 1024:2048]), waits=[bvb_t], inc=True)
            vh_done = []
            for tb in range(NTB):
                for half in range(2):
                    bank = half
                    for kc in range(HEC):
                        S.op("tensor", lambda e, tb=tb, half=half, kc=kc, bank=bank:
                             e.matmul(
                                 PSA[:, bank * 512:(bank + 1) * 512],
                                 qkv[16 + kc][:, tb * P:(tb + 1) * P],
                                 wmat[:, kc * 1024 + half * 512:
                                      kc * 1024 + (half + 1) * 512],
                                 start=(kc == 0), stop=(kc == HEC - 1)),
                             waits=[wv_tok, bv_ev] +
                                   ([vh_done[-2]] if len(vh_done) >= 2 else []),
                             inc=(kc == HEC - 1))
                    mt_ = S.tok("tensor")
                    vh_done.append(S.op("vector",
                        lambda e, tb=tb, half=half, bank=bank: e.tensor_tensor(
                            out=vh_tm[tb][:, half * 512:(half + 1) * 512],
                            in0=PSA[:, bank * 512:(bank + 1) * 512],
                            in1=bv_bc[:, half * 512:(half + 1) * 512], op=ALU.add),
                        waits=[mt_], inc=True))
            vh_all = vh_done[-1]
            if stop == "vh" + sfx:
                emit_export("qh", carve(AR2, 24576, P, HEC * MT, BF16), P,
                            HEC * MT, True, [vh_all])
                emit_export("kh", carve(AR2, 32768, P, HEC * MT, BF16), P,
                            HEC * MT, True, [vh_all])
                emit_export("vh", carve(AR2, 40960, P, NTB * W_, BF16), P,
                            NTB * W_, True, [vh_all])
                stopped = True
                break

            od_tok = None
            for hc in range(HEC):
                od_tok = qX.dma(lambda e, hc=hc: e.dma_start(
                    out=qh_od[hc][:], in_=qh_s[hc][64:128, :]),
                    waits=[qh_tok, vh_all])
                od_tok = qX.dma(lambda e, hc=hc: e.dma_start(
                    out=kh_od[hc][:], in_=kh_s[hc][64:128, :]),
                    waits=[kh_tok, vh_all])
            S.op(g, lambda e: e.memset(mask16, 0.0), waits=[vh_all, S.tok("tensor")])
            mk = S.op(g, lambda e: e.affine_select(
                out=mask16, in_=mask16, compare_op=ALU.is_ge, fill=NEG,
                base=0, pattern=[[0, H], [-1, P]], channel_multiplier=1), inc=True)
            attn_last = None
            for tb in range(NTB):
                for hh in range(H):
                    hc, hp = hh // 2, (hh % 2) * 64
                    qsl = (qh_s[hc][0:64, tb * P:(tb + 1) * P] if hp == 0
                           else qh_od[hc][:, tb * P:(tb + 1) * P])
                    ksl = (kh_s[hc][0:64, tb * P:(tb + 1) * P] if hp == 0
                           else kh_od[hc][:, tb * P:(tb + 1) * P])
                    S.op("tensor", lambda e, hh=hh, qsl=qsl, ksl=ksl: e.matmul(
                        PSA[:, hh * P:(hh + 1) * P], qsl, ksl,
                        start=True, stop=True, skip_group_check=True),
                        waits=[mk, vh_all, od_tok] +
                              ([attn_last] if attn_last else []),
                        inc=(hh == H - 1))
                sc_t = S.tok("tensor")
                v_t = S.op("vector", lambda e: e.tensor_tensor(
                    out=attn_sc_v, in0=PSA[:, 0:2048], in1=mask16, op=ALU.add),
                    waits=[sc_t], inc=True)
                v_t = S.op("vector", lambda e: e.tensor_reduce(
                    out=attn_mx[:, 0:H],
                    in_=attn_sc_v.rearrange("p (h k) -> p h k", h=H),
                    axis=AX.X, op=ALU.max, negate=True), waits=[v_t], inc=True)
                for hh in range(H):
                    S.op("scalar", lambda e, hh=hh: e.activation(
                        out=attn_e_v[:, hh * P:(hh + 1) * P],
                        in_=attn_sc_v[:, hh * P:(hh + 1) * P],
                        func=AF.Exp, bias=attn_mx[:, hh:hh + 1]),
                        waits=[v_t], inc=(hh == H - 1))
                s_t = S.tok("scalar")
                v_t = S.op("vector", lambda e: e.tensor_reduce(
                    out=attn_mx[:, H:2 * H],
                    in_=attn_e_v.rearrange("p (h k) -> p h k", h=H),
                    axis=AX.X, op=ALU.add), waits=[s_t], inc=True)
                v_t = S.op("vector", lambda e: e.reciprocal(
                    attn_mx[:, 2 * H:3 * H], attn_mx[:, H:2 * H]),
                    waits=[v_t], inc=True)
                for hh in range(H):
                    S.op("vector", lambda e, hh=hh: e.tensor_scalar(
                        out=attn_p_v[:, hh * P:(hh + 1) * P],
                        in0=attn_e_v[:, hh * P:(hh + 1) * P],
                        scalar1=attn_mx[:, 2 * H + hh:2 * H + hh + 1], scalar2=None,
                        op0=ALU.mult), waits=[v_t], inc=(hh == H - 1))
                v_t = S.tok("vector")
                for hh in range(H):
                    S.op("tensor", lambda e, hh=hh: e.transpose(
                        PSAb[:, 4096 + hh * P: 4096 + (hh + 1) * P],
                        attn_p_v[:, hh * P:(hh + 1) * P], identb[:]),
                        waits=[v_t], inc=(hh == H - 1))
                t_t = S.tok("tensor")
                s_t = S.op("scalar", lambda e: e.activation(
                    out=attn_pT_v, in_=PSAb[:, 4096:6144], func=AF.Copy),
                    waits=[t_t], inc=True)
                for hh in range(H):
                    hc, hp = hh // 2, (hh % 2) * 64
                    S.op("tensor", lambda e, hh=hh, hc=hc, hp=hp, tb=tb: e.matmul(
                        PSA[hp:hp + 64, 3072 + hc * P: 3072 + (hc + 1) * P],
                        vh_tm[tb][:, hh * 64:(hh + 1) * 64],
                        attn_pT_v[:, hh * P:(hh + 1) * P],
                        start=True, stop=True, skip_group_check=True),
                        waits=[s_t], inc=(hh == H - 1))
                t_t = S.tok("tensor")
                attn_last = S.op("scalar", lambda e, tb=tb: e.activation(
                    out=o_fm_all.rearrange("p (c n) -> p c n", c=HEC)[
                        :, :, tb * P:(tb + 1) * P],
                    in_=PSA[:, 3072:4096].rearrange("p (c n) -> p c n", c=HEC),
                    func=AF.Copy), waits=[t_t], inc=True)
                if stop == "attn1" + sfx and tb == 0:
                    break
            if stop == "attn1" + sfx:
                emit_export("qod", carve(AR2, 81920, 64, HEC * MT, BF16), 64,
                            HEC * MT, True, [attn_last])
                emit_export("sc", attn_sc_v, P, H * P, False, [attn_last])
                emit_export("mx", attn_mx[:], P, 3 * H, False, [attn_last])
                emit_export("pv", attn_p_v, P, H * P, True, [attn_last])
                emit_export("pT", attn_pT_v, P, H * P, True, [attn_last])
                emit_export("o1", o_fm_all, P, HEC * MT, True, [attn_last])
                stopped = True
                break
            if stop == "o_fm" + sfx:
                emit_export("o_fm", o_fm_all, P, HEC * MT, True, [attn_last])
                stopped = True
                break

            wo_tok = load_wmat(wo_in, [attn_last])
            ao = qh_s
            ao_done = []
            for wc in range(HEC):
                bank = wc % 2
                for hc in range(HEC):
                    S.op("tensor", lambda e, wc=wc, hc=hc, bank=bank: e.matmul(
                        PSA[:, bank * 512:(bank + 1) * 512],
                        wmat[:, hc * 1024 + wc * P: hc * 1024 + (wc + 1) * P],
                        o_fm[hc], start=(hc == 0), stop=(hc == HEC - 1)),
                        waits=[wo_tok] + ([ao_done[wc - 2]] if wc >= 2 else []),
                        inc=(hc == HEC - 1))
                mt_ = S.tok("tensor")
                ao_done.append(S.op("scalar", lambda e, wc=wc, bank=bank:
                                    e.activation(
                    out=ao[wc], in_=PSA[:, bank * 512:(bank + 1) * 512],
                    func=AF.Identity, bias=biases[:, 40 + wc:40 + wc + 1]),
                    waits=[mt_], inc=True))
            ao_all = ao_done[-1]
            if stop == "ao" + sfx:
                emit_export("ao", carve(AR2, 24576, P, HEC * MT, BF16), P,
                            HEC * MT, True, [ao_all])
                stopped = True
                break

            pj_toks = {}
            at_done = []

            def pj_dma(nc_i, waits):
                buf = pjw_b[nc_i % 2]
                tok = None
                for c in range(HEC):
                    tok = qW.dma(lambda e, nc_i=nc_i, buf=buf, c=c: e.dma_start(
                        out=buf[:, c * P:(c + 1) * P],
                        in_=projw[c * P:(c + 1) * P, nc_i * P:(nc_i + 1) * P]),
                        waits=(waits if c == 0 else ()))
                pj_toks[nc_i] = tok

            pj_dma(0, [ao_all])
            pj_dma(1, [ao_all])
            for nc_i in range(KC):
                buf = pjw_b[nc_i % 2]
                bank = nc_i % 2
                for kc in range(HEC):
                    S.op("tensor", lambda e, nc_i=nc_i, kc=kc, buf=buf,
                         bank=bank: e.matmul(
                             PSA[:, bank * 512:(bank + 1) * 512],
                             buf[:, kc * P:(kc + 1) * P],
                             ao[kc],
                             start=(kc == 0), stop=(kc == HEC - 1)),
                         waits=[pj_toks[nc_i]] +
                               ([at_done[nc_i - 2]] if nc_i >= 2 else []),
                         inc=(kc == HEC - 1))
                mt_ = S.tok("tensor")
                if nc_i + 2 < KC:
                    pj_dma(nc_i + 2, [mt_])
                stg = at_st[nc_i % 2]
                ev = S.op("scalar", lambda e, nc_i=nc_i, bank=bank, stg=stg:
                          e.activation(
                              out=stg, in_=PSA[:, bank * 512:(bank + 1) * 512],
                              func=AF.Identity, bias=biases[:, 48 + nc_i:48 + nc_i + 1]),
                          waits=[mt_], inc=True)
                cp = S.op("scalar", lambda e, stg=stg: e.activation(
                    out=p7_cp, in_=stg, func=AF.Copy), waits=[ev], inc=True)
                sq = S.op("scalar", lambda e, stg=stg: e.activation(
                    out=p7_sq, in_=stg, func=AF.Square), waits=[cp], inc=True)
                S.op("tensor", lambda e, nc_i=nc_i: e.matmul(
                    PSA[0:1, 1024:1536], onescol_b[:], p7_cp[:],
                    start=(nc_i == 0), stop=(nc_i == KC - 1), skip_group_check=True),
                    waits=[cp])
                S.op("tensor", lambda e, nc_i=nc_i: e.matmul(
                    PSA[0:1, 1536:2048], onescol_b[:], p7_sq[:],
                    start=(nc_i == 0), stop=(nc_i == KC - 1), skip_group_check=True),
                    waits=[sq], inc=True)
                at_done.append(qY.dma(lambda e, nc_i=nc_i, stg=stg: e.dma_start(
                    out=at_sp[nc_i * P:(nc_i + 1) * P, :], in_=stg),
                    waits=[ev, S.tok("tensor")]))
            if stop == "attn_sp" + sfx:
                S.emit("sync", lambda e: e.nop(), waits=[at_done[-1]])
                stopped = True
                break

            bc2 = ln_row_scalars(PSA[0:1, 1024:1536], PSA[0:1, 1536:2048],
                                 BK(6), BK(7),
                                 [S.tok("tensor"), at_done[-1]])
            y1_done = []
            for nc_i in range(KC):
                rlw = [bc2] + ([y1_done[nc_i - 2]] if nc_i >= 2 else [])
                rt = qY.dma(lambda e, nc_i=nc_i: e.dma_start(
                    out=at_st[nc_i % 2], in_=at_sp[nc_i * P:(nc_i + 1) * P, :]),
                    waits=rlw)
                for tb in range(NTB):
                    S.op("tensor", lambda e, tb=tb, nc_i=nc_i: e.transpose(
                        PSA[:, 2048 + tb * P: 2048 + (tb + 1) * P],
                        x_blk[tb][:, nc_i * P:(nc_i + 1) * P], identf[:]),
                        waits=[bc2] + ([y1_done[nc_i - 2]] if nc_i >= 2 else []),
                        inc=(tb == NTB - 1))
                tx = S.tok("tensor")
                evx = S.op("vector", lambda e: e.tensor_copy(
                    out=xfm_st, in_=PSA[:, 2048:2560]), waits=[tx], inc=True)
                v1 = S.op("vector", lambda e, nc_i=nc_i: e.tensor_tensor(
                    out=y1_st, in0=at_st[nc_i % 2], in1=PSA[:, 3072:3584],
                    op=ALU.mult), waits=[rt, bc2], inc=True)
                v1 = S.op("vector", lambda e: e.tensor_tensor(
                    out=y1_st, in0=y1_st, in1=PSA[:, 3584:4096], op=ALU.add),
                    waits=[v1], inc=True)
                v1 = S.op("vector", lambda e: e.tensor_tensor(
                    out=y1_st, in0=y1_st, in1=xfm_st, op=ALU.add),
                    waits=[v1, evx], inc=True)
                cp = S.op("scalar", lambda e: e.activation(
                    out=p7_cp, in_=y1_st, func=AF.Copy), waits=[v1], inc=True)
                sq = S.op("scalar", lambda e: e.activation(
                    out=p7_sq, in_=y1_st, func=AF.Square), waits=[cp], inc=True)
                S.op("tensor", lambda e, nc_i=nc_i: e.matmul(
                    PSA[0:1, 1024:1536], onescol_b[:], p7_cp[:],
                    start=(nc_i == 0), stop=(nc_i == KC - 1), skip_group_check=True),
                    waits=[cp])
                S.op("tensor", lambda e, nc_i=nc_i: e.matmul(
                    PSA[0:1, 1536:2048], onescol_b[:], p7_sq[:],
                    start=(nc_i == 0), stop=(nc_i == KC - 1), skip_group_check=True),
                    waits=[sq], inc=True)
                y1_done.append(qY.dma(lambda e, nc_i=nc_i, t=t: e.dma_start(
                    out=y1_sp[t, nc_i * P:(nc_i + 1) * P, :], in_=y1_st),
                    waits=[v1, S.tok("tensor")]))
            if stop == "y1" + sfx:
                S.emit("sync", lambda e: e.nop(), waits=[y1_done[-1]])
                stopped = True
                break

            bc3 = ln_row_scalars(PSA[0:1, 1024:1536], PSA[0:1, 1536:2048],
                                 BK(6), BK(7),
                                 [S.tok("tensor"), y1_done[-1]])
            x2_done = []
            for nc_i in range(KC):
                buf = y1_ld[nc_i % 2]
                rlw = [bc3] + ([x2_done[nc_i - 2]] if nc_i >= 2 else [])
                rt = qY.dma(lambda e, nc_i=nc_i, buf=buf, t=t: e.dma_start(
                    out=buf, in_=y1_sp[t, nc_i * P:(nc_i + 1) * P, :]), waits=rlw)
                v1 = S.op("vector", lambda e, buf=buf: e.tensor_tensor(
                    out=y1_st, in0=buf, in1=PSA[:, 3072:3584], op=ALU.mult),
                    waits=[rt], inc=True)
                x2_done.append(S.op("vector", lambda e, nc_i=nc_i: e.tensor_tensor(
                    out=actT[:, nc_i * ACTW: nc_i * ACTW + MT], in0=y1_st,
                    in1=PSA[:, 3584:4096], op=ALU.add), waits=[v1], inc=True))
            x2_all = x2_done[-1]
            if stop == "xln2T" + sfx:
                emit_export("xln2T", actT, P, KC * ACTW, True, [x2_all])
                stopped = True
                break

            for kc in range(KC):
                S.op("tensor", lambda e, kc=kc: e.matmul(
                    PSA[0:KC, 0:512], c32[:, 32 - kc:64 - kc],
                    actT[:, kc * ACTW: kc * ACTW + MT],
                    start=(kc == 0), stop=(kc == KC - 1)),
                    waits=[x2_all], inc=(kc == KC - 1))
            xg_t = S.tok("tensor")
            xg_tok = S.op("scalar", lambda e: e.activation(
                out=xg_sb[:], in_=PSA[0:KC, 0:512], func=AF.Copy),
                waits=[xg_t], inc=True)

            S.op(g, lambda e: e.memset(sel32, 1.0), waits=[x2_all])
            S.op(g, lambda e: e.affine_select(
                out=sel32, in_=sel32, compare_op=ALU.is_ge, fill=0.0,
                base=0, pattern=[[1, KC * P]], channel_multiplier=-P))
            sel_t = S.op(g, lambda e: e.affine_select(
                out=sel32, in_=sel32, compare_op=ALU.is_ge, fill=0.0,
                base=127, pattern=[[-1, KC * P]], channel_multiplier=P), inc=True)

            win_free = {4: None, 5: None}
            prev_ext = {0: None, 1: None}
            h_done = []
            for ns in range(FC):
                pair_tok = {}
                for mi, (bsrc, ssrc, zsrc) in enumerate(
                        ((ub_in, us_in, zsu_d), (gb_in, gs_in, zsg_d))):
                    bi = mi
                    bw = [sel_t, xg_tok, S.tok("vector")]
                    if prev_ext[bi] is not None:
                        bw.append(prev_ext[bi])
                    bt = qB.dma(lambda e, bsrc=bsrc, ns=ns, bi=bi: e.dma_start(
                        out=gu_byte[bi], in_=bsrc[ns]), waits=bw)
                    st_ = qW.dma(lambda e, ssrc=ssrc, ns=ns, mi=mi: e.dma_start(
                        out=ssl[mi], in_=ssrc[:, ns * P:(ns + 1) * P]),
                        waits=[S.tok("tensor")])
                    zt_ = qB.dma(lambda e, zsrc=zsrc, ns=ns, mi=mi: e.dma_start(
                        out=zsl[mi], in_=zsrc[:, ns * P:(ns + 1) * P]),
                        waits=[S.tok("tensor")])
                    et = ext4("vector", gu_byte[bi], gu_nib[bi], KC * 16,
                              [bt, S.tok("vector")])
                    prev_ext[bi] = et
                    win = 4 + mi
                    for q8 in range(8):
                        for j4 in range(4):
                            kc = q8 * 4 + j4
                            bcw = [st_, sel_t]
                            if win_free[win] is not None:
                                bcw.append(win_free[win])
                            S.op("tensor", lambda e, kc=kc, j4=j4, win=win, mi=mi:
                                 e.matmul(
                                     PSA[:, win * 512 + j4 * P:
                                         win * 512 + (j4 + 1) * P],
                                     sel32[:, kc * P:(kc + 1) * P], ssl[mi][:],
                                     start=True, stop=True, skip_group_check=True),
                                 waits=bcw, inc=(j4 == 3))
                        bc_t = S.tok("tensor")
                        mtok = S.op("vector", lambda e, q8=q8, bi=bi, win=win:
                                    e.tensor_tensor(
                                out=gu_w[bi][:, q8 * 512:(q8 + 1) * 512],
                                in0=gu_nib[bi][:, q8 * 512:(q8 + 1) * 512],
                                in1=PSA[:, win * 512:(win + 1) * 512], op=ALU.mult),
                                waits=[et, bc_t], inc=True)
                        win_free[win] = mtok
                    wq_done = S.tok("vector")
                    accb = mi
                    S.op("tensor", lambda e, mi=mi, accb=accb: e.matmul(
                        PSA[:, accb * 512:(accb + 1) * 512],
                        zsl[mi][:], xg_sb[:],
                        start=True, stop=False, skip_group_check=True),
                        waits=[zt_, xg_tok, wq_done, S.tok("scalar")])
                    for kc in range(KC):
                        S.op("tensor", lambda e, kc=kc, bi=bi, accb=accb: e.matmul(
                            PSA[:, accb * 512:(accb + 1) * 512],
                            gu_w[bi][:, kc * P:(kc + 1) * P],
                            actT[:, kc * ACTW: kc * ACTW + MT],
                            start=False, stop=(kc == KC - 1),
                            skip_group_check=True),
                            waits=[wq_done], inc=(kc == KC - 1))
                    pair_tok[mi] = S.tok("tensor")
                sl = S.op("scalar", lambda e: e.activation(
                    out=silu_sb, in_=PSA[:, 512:1024], func=AF.Silu),
                    waits=[pair_tok[1]], inc=True)
                h_done.append(S.op("vector", lambda e, ns=ns: e.tensor_tensor(
                    out=h_fm[ns], in0=silu_sb, in1=PSA[:, 0:512], op=ALU.mult),
                    waits=[sl, pair_tok[0]], inc=True))
            h_all = h_done[-1]
            if stop == "h" + sfx:
                emit_export("h", carve(AR2, 0, P, FC * MT, BF16), P, FC * MT,
                            True, [h_all])
                stopped = True
                break

            for kc in range(FC):
                S.op("tensor", lambda e, kc=kc: e.matmul(
                    PSA[0:FC, 0:512], c86[:, 86 - kc:172 - kc], h_fm[kc],
                    start=(kc == 0), stop=(kc == FC - 1)),
                    waits=[h_all], inc=(kc == FC - 1))
            hg_t = S.tok("tensor")
            hg_tok = S.op("scalar", lambda e: e.activation(
                out=hg_sb[:], in_=PSA[0:FC, 0:512], func=AF.Copy),
                waits=[hg_t], inc=True)

            win_free = {4: None, 5: None}
            prev_extd = {0: None, 1: None}
            ffn_done = []
            for ncg in range(8):
                first_sc = S.tok("scalar")
                for kc in range(FC):
                    bi = kc % 2
                    bw = [hg_tok, S.tok("vector")]
                    if prev_extd[bi] is not None:
                        bw.append(prev_extd[bi])
                    bt = qB.dma(lambda e, ncg=ncg, kc=kc, bi=bi: e.dma_start(
                        out=dn_byte[bi][:, 0:256], in_=db_in[ncg, kc]), waits=bw)
                    st_ = qW.dma(lambda e, ncg=ncg, kc=kc, bi=bi: e.dma_start(
                        out=dsl[bi], in_=ds_in[kc:kc + 1,
                                              ncg * 512:(ncg + 1) * 512]),
                        waits=[S.tok("tensor")])
                    et = ext4("vector", dn_byte[bi][:, 0:256], dn_nib[bi], 64,
                              [bt, S.tok("vector")])
                    prev_extd[bi] = et
                    win = 4 + bi
                    bcw = [st_]
                    if win_free[win] is not None:
                        bcw.append(win_free[win])
                    S.op("tensor", lambda e, bi=bi, win=win: e.matmul(
                        PSA[:, win * 512:(win + 1) * 512], onesr_b[:], dsl[bi][:],
                        start=True, stop=True, skip_group_check=True),
                        waits=bcw, inc=True)
                    bc_t = S.tok("tensor")
                    wd_done = S.op("vector", lambda e, bi=bi, win=win:
                                   e.tensor_tensor(
                        out=dn_w[bi][:], in0=dn_nib[bi][:],
                        in1=PSA[:, win * 512:(win + 1) * 512], op=ALU.mult),
                        waits=[et, bc_t], inc=True)
                    win_free[win] = wd_done
                    for nc4 in range(4):
                        S.op("tensor", lambda e, kc=kc, nc4=nc4, bi=bi: e.matmul(
                            PSA[:, nc4 * 512:(nc4 + 1) * 512],
                            dn_w[bi][:, nc4 * P:(nc4 + 1) * P], h_fm[kc],
                            start=(kc == 0), stop=False, skip_group_check=True),
                            waits=[wd_done, hg_tok, first_sc], inc=(nc4 == 3))
                for nc4 in range(4):
                    nc_i = ncg * 4 + nc4
                    zt_ = qB.dma(lambda e, nc_i=nc_i, nc4=nc4: e.dma_start(
                        out=nzd[nc4 % 2], in_=zsd_d[:, nc_i * P:(nc_i + 1) * P]),
                        waits=[S.tok("tensor")])
                    S.op("tensor", lambda e, nc4=nc4: e.matmul(
                        PSA[:, nc4 * 512:(nc4 + 1) * 512],
                        nzd[nc4 % 2][:], hg_sb[:],
                        start=False, stop=True, skip_group_check=True),
                        waits=[zt_, hg_tok], inc=True)
                    mt_ = S.tok("tensor")
                    stg = ffn_stage[nc4 % 2]
                    ev = S.op("scalar", lambda e, nc4=nc4, stg=stg: e.activation(
                        out=stg, in_=PSA[:, nc4 * 512:(nc4 + 1) * 512],
                        func=AF.Copy), waits=[mt_], inc=True)
                    cp = S.op("scalar", lambda e, stg=stg: e.activation(
                        out=st_cp, in_=stg, func=AF.Copy), waits=[ev], inc=True)
                    sq = S.op("scalar", lambda e, stg=stg: e.activation(
                        out=st_sq, in_=stg, func=AF.Square), waits=[cp], inc=True)
                    S.op("tensor", lambda e, nc_i=nc_i: e.matmul(
                        PSA[0:1, 3072:3584], onescol_b[:], st_cp[:],
                        start=(nc_i == 0), stop=(nc_i == KC - 1),
                        skip_group_check=True), waits=[cp])
                    S.op("tensor", lambda e, nc_i=nc_i: e.matmul(
                        PSA[0:1, 3584:4096], onescol_b[:], st_sq[:],
                        start=(nc_i == 0), stop=(nc_i == KC - 1),
                        skip_group_check=True), waits=[sq], inc=True)
                    ffn_done.append(qY.dma(lambda e, nc_i=nc_i, stg=stg:
                                           e.dma_start(
                        out=at_sp[nc_i * P:(nc_i + 1) * P, :], in_=stg),
                        waits=[ev, S.tok("tensor")]))

            if stop == "ffn_sp" + sfx:
                S.emit("sync", lambda e: e.nop(), waits=[ffn_done[-1]])
                stopped = True
                break

            bc4 = ln_row_scalars(PSA[0:1, 3072:3584], PSA[0:1, 3584:4096],
                                 BK(4), BK(5),
                                 [S.tok("tensor"), ffn_done[-1]])
            out_toks = []
            for nc_i in range(KC):
                ow = [bc4] + ([out_toks[nc_i - 2]] if nc_i >= 2 else [])
                f_rt = qY.dma(lambda e, nc_i=nc_i: e.dma_start(
                    out=ffn_ld[nc_i % 2], in_=at_sp[nc_i * P:(nc_i + 1) * P, :]),
                    waits=ow)
                y_rt = qY.dma(lambda e, nc_i=nc_i, t=t: e.dma_start(
                    out=y1_ld[nc_i % 2], in_=y1_sp[t, nc_i * P:(nc_i + 1) * P, :]),
                    waits=ow)
                v1 = S.op("vector", lambda e, nc_i=nc_i: e.tensor_tensor(
                    out=y2_st[nc_i % 2], in0=ffn_ld[nc_i % 2], in1=PSA[:, 2048:2560],
                    op=ALU.mult), waits=[f_rt] + ow, inc=True)
                v1 = S.op("vector", lambda e, nc_i=nc_i: e.tensor_tensor(
                    out=y2_st[nc_i % 2], in0=y2_st[nc_i % 2], in1=PSA[:, 2560:3072],
                    op=ALU.add), waits=[v1], inc=True)
                v1 = S.op("vector", lambda e, nc_i=nc_i: e.tensor_tensor(
                    out=y2_st[nc_i % 2], in0=y2_st[nc_i % 2], in1=y1_ld[nc_i % 2],
                    op=ALU.add), waits=[v1, y_rt], inc=True)
                out_toks.append(qX.dma(
                    lambda e, nc_i=nc_i, tok0=tok0: e.dma_start(
                        out=out_d[nc_i * P:(nc_i + 1) * P, tok0:tok0 + MT],
                        in_=y2_st[nc_i % 2]), waits=[v1]))
            if stop == "tile0" + sfx:
                stopped = True
                break

        fw = qX.final_waits() + qY.final_waits() + qB.final_waits() + qW.final_waits()
        S.emit("sync", lambda e: e.nop(), waits=fw)

        block = st.enter_context(nc.Block())

        def mk_runner(eng):
            items = S.q[eng]

            def run(e):
                for fn, w, ii in items:
                    for sh, v in w:
                        e.wait_ge(sh, v)
                    ins = fn(e)
                    for sh, k in ii:
                        ins.then_inc(sh, k)
            return run

        for eng in ENGINES:
            getattr(block, eng)(mk_runner(eng))
    return nc, dbg



def _prep_core_inputs(core, inp):
    x = np.ascontiguousarray(inp["L0_attn_inputs"].reshape(NCORES * TPC, D))
    g0 = core * TPC
    seq = g0 // SEQ
    s0, s1 = seq * SEQ, (seq + 1) * SEQ
    xc = np.zeros((TPC + 2 * P, D), np.float32)
    lo, hi = max(g0 - P, s0), min(g0 + TPC + P, s1)
    xc[lo - (g0 - P): hi - (g0 - P)] = x[lo:hi]

    u8 = lambda a: np.ascontiguousarray(a).view(np.uint8)
    gb = u8(inp["L0_ffn_gate_qweight"]).reshape(KC, P, FC, 64)
    gb = np.ascontiguousarray(gb.transpose(2, 1, 0, 3).reshape(FC, P, KC * 64))
    ub = u8(inp["L0_ffn_up_qweight"]).reshape(KC, P, FC, 64)
    ub = np.ascontiguousarray(ub.transpose(2, 1, 0, 3).reshape(FC, P, KC * 64))
    dbv = u8(inp["L0_ffn_down_qweight"]).reshape(FC, P, 8, 256)
    dbv = np.ascontiguousarray(dbv.transpose(2, 0, 1, 3))

    m = dict(
        x_in=xc,
        convw=np.ascontiguousarray(inp["L0_attn_conv_w"]),
        convb=np.ascontiguousarray(inp["L0_attn_conv_b"].reshape(CONVO, 1)),
        wq_in=np.ascontiguousarray(inp["L0_attn_wq"].reshape(W_, W_)),
        wk_in=np.ascontiguousarray(inp["L0_attn_wk"].reshape(W_, W_)),
        wv_in=np.ascontiguousarray(inp["L0_attn_wv"].reshape(W_, W_)),
        wo_in=np.ascontiguousarray(inp["L0_attn_wo"].reshape(W_, W_)),
        bq_in=np.ascontiguousarray(inp["L0_attn_bq"].reshape(W_, 1)),
        bk_in=np.ascontiguousarray(inp["L0_attn_bk"].reshape(W_, 1)),
        bv_in=np.ascontiguousarray(inp["L0_attn_bv"].reshape(1, W_)),
        bo_in=np.ascontiguousarray(inp["L0_attn_bo"].reshape(W_, 1)),
        projw=np.ascontiguousarray(inp["L0_attn_proj_w"]),
        projb=np.ascontiguousarray(inp["L0_attn_proj_b"].reshape(KC, P).T),
        gb_in=gb, ub_in=ub, db_in=dbv,
        gs_in=np.ascontiguousarray(inp["L0_ffn_gate_scales"]),
        us_in=np.ascontiguousarray(inp["L0_ffn_up_scales"]),
        ds_in=np.ascontiguousarray(inp["L0_ffn_down_scales"]),
        gz_in=u8(inp["L0_ffn_gate_qzeros"]).reshape(KC, DFF // 2),
        uz_in=u8(inp["L0_ffn_up_qzeros"]).reshape(KC, DFF // 2),
        dz_in=u8(inp["L0_ffn_down_qzeros"]).reshape(FC, D // 2),
    )
    return m


def kernel(**inputs):
    nc, _ = build_core()
    in_maps = [_prep_core_inputs(c, inputs) for c in range(NCORES)]
    res = run_bass_kernel_spmd(nc, in_maps, core_ids=list(range(NCORES)))
    outs = [np.ascontiguousarray(r["out_d"].T) for r in res.results]
    return np.concatenate(outs, axis=0).reshape(4, SEQ, D)


# version 02
# speedup vs baseline: 1.0301x; 1.0301x over previous
import numpy as np
import concourse.bass as bass
import concourse.mybir as mybir
from concourse.bass_utils import run_bass_kernel_spmd
from contextlib import ExitStack

F32 = mybir.dt.float32
BF16 = mybir.dt.bfloat16
U8 = mybir.dt.uint8
AF = mybir.ActivationFunctionType
ALU = mybir.AluOpType
AX = mybir.AxisListType

P = 128
D = 4096
KC = D // P
CONVO = 3072
NCO = CONVO // P
W_ = 1024
HEC = W_ // P
H, DH = 16, 64
DFF = 11008
FC = DFF // P
MT = 512
NTB = MT // P
NT = 2
TPC = NT * MT
NCORES = 8
SEQ = 2048
NEG = -1.0e9
EPS = 1e-5
INV_D = 1.0 / 4096.0

ENGINES = ("tensor", "vector", "scalar", "gpsimd", "sync")

def BK(i, w=512):
    return slice(i * 512, i * 512 + w)


class Sched:
    def __init__(self, nc, st):
        self.nc = nc
        self.st = st
        self.q = {e: [] for e in ENGINES}
        self.sems = {}
        self.cnt = {}
        self.last_wait = {e: {} for e in ENGINES}

    def new_sem(self, name):
        h = self.st.enter_context(self.nc.semaphore(name))
        self.sems[name] = h
        self.cnt[name] = 0
        return name

    @staticmethod
    def _flat(waits):
        out = []
        for w in waits:
            if isinstance(w, tuple):
                out.append(w)
            else:
                out.extend(w)
        return out

    def emit(self, eng, fn, waits=(), incs=()):
        w = []
        for sname, val in self._flat(waits):
            if val <= 0:
                continue
            if self.last_wait[eng].get(sname, -1) >= val:
                continue
            self.last_wait[eng][sname] = val
            w.append((self.sems[sname], val))
        ii = []
        for sname, k in incs:
            self.cnt[sname] += k
            ii.append((self.sems[sname], k))
        self.q[eng].append((fn, w, ii))

    def op(self, eng, fn, waits=(), inc=False):
        sname = "e_" + eng
        self.emit(eng, fn, waits=waits, incs=([(sname, 1)] if inc else ()))
        return (sname, self.cnt[sname])

    def tok(self, eng):
        return ("e_" + eng, self.cnt["e_" + eng])


class DmaQ:
    def __init__(self, S, eng, name):
        self.S = S
        self.eng = eng
        self.sem = [S.new_sem(name + "0"), S.new_sem(name + "1")]
        self.i = 0

    def dma(self, fn, waits=()):
        par = self.i % 2
        self.i += 1
        sname = self.sem[par]
        w = list(waits) + [(sname, self.S.cnt[sname])]
        self.S.emit(self.eng, fn, waits=w, incs=[(sname, 16)])
        return [(s, self.S.cnt[s]) for s in self.sem]

    def final_waits(self):
        return [(s, self.S.cnt[s]) for s in self.sem]


def build_core(stop=None):
    nc = bass.Bass("TRN2", target_bir_lowering=False)
    dr = {}

    def din(name, shape, dtype=F32):
        dr[name] = nc.dram_tensor(name, list(shape), dtype, kind="ExternalInput").ap()
        return dr[name]

    x_in = din("x_in", (TPC + 2 * P, D))
    convw = din("convw", (3, D, CONVO))
    convb = din("convb", (CONVO, 1))
    wq_in = din("wq_in", (W_, W_))
    wk_in = din("wk_in", (W_, W_))
    wv_in = din("wv_in", (W_, W_))
    wo_in = din("wo_in", (W_, W_))
    bq_in = din("bq_in", (W_, 1))
    bk_in = din("bk_in", (W_, 1))
    bv_in = din("bv_in", (1, W_))
    bo_in = din("bo_in", (W_, 1))
    projw = din("projw", (W_, D))
    projb = din("projb", (P, KC))
    gb_in = din("gb_in", (FC, P, KC * 64), U8)
    ub_in = din("ub_in", (FC, P, KC * 64), U8)
    db_in = din("db_in", (8, FC, P, 256), U8)
    gs_in = din("gs_in", (KC, DFF))
    us_in = din("us_in", (KC, DFF))
    ds_in = din("ds_in", (FC, D))
    gz_in = din("gz_in", (KC, DFF // 2), U8)
    uz_in = din("uz_in", (KC, DFF // 2), U8)
    dz_in = din("dz_in", (FC, D // 2), U8)

    out_d = nc.dram_tensor("out_d", [D, TPC], F32, kind="ExternalOutput").ap()
    y1_sp = nc.dram_tensor("y1_spill", [NT, D, MT], F32,
                           kind=("ExternalOutput" if stop == "y1" else "Internal")).ap()
    at_sp = nc.dram_tensor("attn_spill", [D, MT], F32,
                           kind=("ExternalOutput" if stop in ("attn_sp", "ffn_sp")
                                 else "Internal")).ap()
    zsg_d = nc.dram_tensor("zsg_d", [KC, DFF], BF16, kind="Internal").ap()
    zsu_d = nc.dram_tensor("zsu_d", [KC, DFF], BF16, kind="Internal").ap()
    zsd_d = nc.dram_tensor("zsd_d", [FC, D], BF16, kind="Internal").ap()

    dbg = {}
    st = ExitStack()
    with st:
        AR1 = st.enter_context(nc.sbuf_tensor("AR1", [P, 66560], U8))
        AR2 = st.enter_context(nc.sbuf_tensor("AR2", [P, 90112], U8))
        AR3 = st.enter_context(nc.sbuf_tensor("AR3", [P, 33536], U8))
        identf = st.enter_context(nc.sbuf_tensor("identf", [P, P], F32))
        identb = st.enter_context(nc.sbuf_tensor("identb", [P, P], BF16))
        onesr = st.enter_context(nc.sbuf_tensor("onesr", [1, P], F32))
        onesr_b = st.enter_context(nc.sbuf_tensor("onesr_b", [1, P], BF16))
        onescol_b = st.enter_context(nc.sbuf_tensor("onescol_b", [P, 1], BF16))
        c32 = st.enter_context(nc.sbuf_tensor("c32", [P, 64], BF16))
        c86 = st.enter_context(nc.sbuf_tensor("c86", [P, 172], BF16))
        convbuf = st.enter_context(nc.sbuf_tensor("convbuf", [P, 2 * 1024], BF16))
        biases = st.enter_context(nc.sbuf_tensor("biases", [P, 80], F32))
        lnsc = st.enter_context(nc.sbuf_tensor("lnsc", [P, 16], F32))
        attn_mx = st.enter_context(nc.sbuf_tensor("attn_mx", [P, 3 * H], F32))
        xg_sb = st.enter_context(nc.sbuf_tensor("xg_sb", [KC, MT], BF16))
        hg_sb = st.enter_context(nc.sbuf_tensor("hg_sb", [FC, MT], BF16))
        r512 = st.enter_context(nc.sbuf_tensor("r512", [1, 4 * MT], F32))
        constt = st.enter_context(nc.sbuf_tensor("constt", [P, 2], F32))
        PSA_t = st.enter_context(nc.psum_tensor("PSA", [P, 4096], F32))
        PSA = PSA_t[:]
        PSAb = PSA_t[:].bitcast(BF16)

        S = Sched(nc, st)
        for e in ENGINES:
            S.new_sem("e_" + e)
        qX = DmaQ(S, "sync", "dX")
        qY = DmaQ(S, "sync", "dY")
        qB = DmaQ(S, "sync", "dB")
        qW = DmaQ(S, "gpsimd", "dW")

        def carve(arena, off_bytes, pcount, fcount, dtype):
            esz = {F32: 4, BF16: 2, U8: 1}[dtype]
            ap = arena[:].bitcast(dtype)
            assert off_bytes % esz == 0, (off_bytes, dtype)
            o = off_bytes // esz
            assert (o + fcount) * esz <= arena.shape[1], \
                f"carve overflow: {off_bytes}+{fcount * esz} > {arena.shape[1]}"
            return ap[0:pcount, o:o + fcount]

        x_blk = [carve(AR1, b * 16384, P, D, F32) for b in range(NTB)]
        gu_byte = [carve(AR1, i * 2048, P, 2048, U8) for i in range(2)]
        gu_nib = [carve(AR1, 4096 + i * 4096, P, 4096, U8) for i in range(2)]
        gu_w = [carve(AR1, 12288 + i * 8192, P, 4096, BF16) for i in range(2)]
        sel32 = carve(AR1, 28672, KC, KC * P, BF16)
        ssl = [carve(AR1, 36864 + i * 256, KC, P, BF16) for i in range(2)]
        zsl = [carve(AR1, 37376 + i * 256, KC, P, BF16) for i in range(2)]
        silu_sb = carve(AR1, 38912, P, MT, F32)
        zb_scr = carve(AR1, 0, KC, DFF // 2, U8)
        zn_scr = carve(AR1, 5504, KC, DFF, U8)
        zs_scr = carve(AR1, 16512, KC, DFF // 4, BF16)
        zb86_scr = carve(AR1, 0, FC, D // 2, U8)
        zn86_scr = carve(AR1, 5504, FC, D, U8)
        zs86_scr = carve(AR1, 16512, FC, D // 2, BF16)
        dn_byte = [carve(AR1, i * 1024, P, 1024, U8) for i in range(2)]
        dn_nib = [carve(AR1, 2048 + i * 512, P, 512, U8) for i in range(2)]
        dn_w = [carve(AR1, 3072 + i * 1024, P, 512, BF16) for i in range(2)]
        dsl = [carve(AR1, 5120 + i * 1024, 1, MT, BF16) for i in range(2)]
        nzd = [carve(AR1, 7168 + i * 256, FC, P, BF16) for i in range(2)]
        st_sq = carve(AR1, 8192, P, MT, BF16)
        st_cp = carve(AR1, 9216, P, MT, BF16)
        ffn_stage = [carve(AR1, 10240 + i * 2048, P, MT, F32) for i in range(2)]

        ACTW = 516
        actT = carve(AR3, 0, P, KC * ACTW, BF16)

        h_fm = [carve(AR2, k * 1024, P, MT, BF16) for k in range(FC)]
        qkv = [carve(AR2, i * 1024, P, MT, BF16) for i in range(NCO)]
        qh_s = [carve(AR2, 24576 + i * 1024, P, MT, BF16) for i in range(HEC)]
        kh_s = [carve(AR2, 32768 + i * 1024, P, MT, BF16) for i in range(HEC)]
        vh_tm = [carve(AR2, 40960 + tb * 2048, P, W_, BF16) for tb in range(NTB)]
        o_fm = [carve(AR2, 49152 + i * 1024, P, MT, BF16) for i in range(HEC)]
        o_fm_all = carve(AR2, 49152, P, HEC * MT, BF16)
        wmat = carve(AR2, 57344, P, HEC * 1024, BF16)
        wmat2 = carve(AR2, 0, P, HEC * 1024, BF16)
        ao_buf = [carve(AR2, 69632 + i * 1024, P, MT, BF16) for i in range(HEC)]
        mask16 = carve(AR2, 73728, P, H * P, F32)
        bv_bc = carve(AR2, 81920, P, W_, F32)
        bvrow = carve(AR2, 86016, 1, W_, F32)
        xh_blk = carve(AR2, 0, P, D, F32)
        sqjunk = carve(AR2, 16384, P, D, BF16)
        xln_scr = carve(AR2, 24576, P, D, BF16)
        attn_sc_v = carve(AR2, 0, P, H * P, F32)
        attn_e_v = carve(AR2, 8192, P, H * P, F32)
        attn_p_v = carve(AR2, 16384, P, H * P, BF16)
        attn_pT_v = carve(AR2, 20480, P, H * P, BF16)
        qh_od = [carve(AR2, 81920 + i * 1024, 64, MT, BF16) for i in range(HEC)]
        kh_od = [carve(AR2, 57344 + i * 1024, 64, MT, BF16) for i in range(HEC)]
        af_buf = [carve(AR2, nc * 2048, P, MT, F32) for nc in range(KC)]
        pjw_b = [carve(AR2, 65536 + i * 2048, P, HEC * P, BF16) for i in range(2)]
        xfm_st = carve(AR2, 77824, P, MT, F32)
        y1_st = carve(AR2, 79872, P, MT, F32)
        y1_ld = [carve(AR2, 81920 + i * 2048, P, MT, F32) for i in range(2)]
        p7_cp = carve(AR2, 86016, P, MT, BF16)
        p7_sq = carve(AR2, 87040, P, MT, BF16)
        y2_st = [carve(AR2, 65536 + i * 2048, P, MT, F32) for i in range(2)]
        ff_buf = [carve(AR1, 24576 + nc * 2048, P, MT, F32) if nc < 16 else
                  carve(AR3, (nc - 16) * 2048, P, MT, F32) for nc in range(KC)]

        g = "gpsimd"

        export_chain = [None]

        def emit_export(name, ap, pcount, fcount, is_bf16, waits, tmp_off=40960):
            tpn = "dbg_" + name
            tp = nc.dram_tensor(tpn, [pcount, fcount], F32, kind="ExternalOutput").ap()
            if not is_bf16:
                tok = qX.dma(lambda e: e.dma_start(out=tp, in_=ap), waits=waits)
                dbg[name] = tpn
                export_chain[0] = tok
                return
            step = 2048
            tmp = carve(AR1, tmp_off, P, step, F32)
            last = export_chain[0]
            for o in range(0, fcount, step):
                wdt = min(step, fcount - o)
                ct = S.op("scalar", lambda e, o=o, wdt=wdt, ap=ap: e.activation(
                    out=tmp[0:pcount, 0:wdt], in_=ap[0:pcount, o:o + wdt],
                    func=AF.Copy), waits=list(waits) + ([last] if last else []),
                    inc=True)
                last = qX.dma(lambda e, o=o, wdt=wdt: e.dma_start(
                    out=tp[0:pcount, o:o + wdt], in_=tmp[0:pcount, 0:wdt]),
                    waits=[ct])
            dbg[name] = tpn
            export_chain[0] = last

        for ident, dt in ((identf, F32), (identb, BF16)):
            S.op(g, lambda e, ident=ident: e.memset(ident[:], 0.0))
            S.op(g, lambda e, ident=ident: e.affine_select(
                out=ident[:], in_=ident[:], compare_op=ALU.not_equal, fill=1.0,
                base=0, pattern=[[-1, P]], channel_multiplier=1))
        S.op(g, lambda e: e.memset(onesr[:], 1.0))
        S.op(g, lambda e: e.memset(onesr_b[:], 1.0))
        S.op(g, lambda e: e.memset(onescol_b[:], 1.0))
        S.op(g, lambda e: e.memset(constt[:, 0:1], EPS))
        S.op(g, lambda e: e.memset(constt[:, 1:2], 0.125))
        S.op(g, lambda e: e.memset(c32[:], 0.0))
        S.op(g, lambda e: e.memset(c32[:, 32:33], 1.0))
        S.op(g, lambda e: e.memset(c86[:], 0.0))
        kconst = S.op(g, lambda e: e.memset(c86[:, 86:87], 1.0), inc=True)

        last_b = None
        for ncx in range(NCO):
            last_b = qW.dma(lambda e, ncx=ncx: e.dma_start(
                out=biases[:, ncx:ncx + 1], in_=convb[ncx * P:(ncx + 1) * P, :]))
        for i, src in ((0, bq_in), (1, bk_in), (2, bo_in)):
            for hc in range(HEC):
                last_b = qW.dma(lambda e, i=i, hc=hc, src=src: e.dma_start(
                    out=biases[:, 24 + i * 8 + hc:24 + i * 8 + hc + 1],
                    in_=src[hc * P:(hc + 1) * P, :]))
        last_b = qW.dma(lambda e: e.dma_start(
            out=biases[:, 48:80], in_=projb))
        bias_tok = S.op("vector", lambda e: e.tensor_scalar(
            out=biases[:, 24:32], in0=biases[:, 24:32], scalar1=0.125, scalar2=None,
            op0=ALU.mult), waits=[last_b], inc=True)

        def ext4(eng, byte_ap, nib_ap, Wrd, waits, inc_last=True):
            bq = byte_ap.rearrange("p (w t f) -> p w t f", w=Wrd, t=2, f=2)
            nv = nib_ap.rearrange("p (w q t) -> p w q t", w=Wrd, q=4, t=2)
            ops = [(ALU.bitwise_and, 15, 0, 0),
                   (ALU.bitwise_and, 15, 1, 2),
                   (ALU.logical_shift_right, 4, 0, 1),
                   (ALU.logical_shift_right, 4, 1, 3)]
            tok = None
            for idx, (alu, scl, fsel, qb) in enumerate(ops):
                iv = bq[:, :, :, fsel:fsel + 1]
                ov = nv[:, :, qb:qb + 1, :]
                tok = S.op(eng, lambda e, iv=iv, ov=ov, alu=alu, scl=scl:
                           e.tensor_single_scalar(out=ov, in_=iv, scalar=scl, op=alu),
                           waits=waits, inc=(inc_last and idx == 3))
            return tok

        for (zin, sin, zout) in ((gz_in, gs_in, zsg_d), (uz_in, us_in, zsu_d)):
            zb_tok = qX.dma(lambda e, zin=zin: e.dma_start(out=zb_scr, in_=zin),
                            waits=[S.tok("vector")])
            etok = ext4("vector", zb_scr, zn_scr, DFF // 8, [zb_tok, kconst])
            for q4 in range(4):
                w0, w1 = q4 * (DFF // 4), (q4 + 1) * (DFF // 4)
                s_tok = qW.dma(lambda e, sin=sin, w0=w0, w1=w1: e.dma_start(
                    out=zs_scr, in_=sin[:, w0:w1]), waits=[S.tok("vector")])
                m_tok = S.op("vector", lambda e, w0=w0, w1=w1: e.tensor_tensor(
                    out=zs_scr, in0=zn_scr[:, w0:w1], in1=zs_scr, op=ALU.mult),
                    waits=[etok, s_tok], inc=True)
                m_tok = S.op("vector", lambda e: e.tensor_scalar(
                    out=zs_scr, in0=zs_scr, scalar1=-1.0, scalar2=None, op0=ALU.mult),
                    waits=[m_tok], inc=True)
                qY.dma(lambda e, zout=zout, w0=w0, w1=w1: e.dma_start(
                    out=zout[:, w0:w1], in_=zs_scr), waits=[m_tok])
        zb_tok = qX.dma(lambda e: e.dma_start(out=zb86_scr, in_=dz_in),
                        waits=[S.tok("vector")])
        etok = ext4("vector", zb86_scr, zn86_scr, D // 8, [zb_tok, kconst])
        for q2 in range(2):
            w0, w1 = q2 * (D // 2), (q2 + 1) * (D // 2)
            s_tok = qW.dma(lambda e, w0=w0, w1=w1: e.dma_start(
                out=zs86_scr, in_=ds_in[:, w0:w1]), waits=[S.tok("vector")])
            m_tok = S.op("vector", lambda e, w0=w0, w1=w1: e.tensor_tensor(
                out=zs86_scr, in0=zn86_scr[:, w0:w1], in1=zs86_scr, op=ALU.mult),
                waits=[etok, s_tok], inc=True)
            m_tok = S.op("vector", lambda e: e.tensor_scalar(
                out=zs86_scr, in0=zs86_scr, scalar1=-1.0, scalar2=None, op0=ALU.mult),
                waits=[m_tok], inc=True)
            qY.dma(lambda e, w0=w0, w1=w1: e.dma_start(
                out=zsd_d[:, w0:w1], in_=zs86_scr), waits=[m_tok])
        zs_ready = qY.final_waits()

        def ln_stats_tm(T_ap, waits):
            v, s = "vector", "scalar"
            t1 = S.op(v, lambda e: e.reduce_sum(lnsc[:, 2:3], T_ap, axis=AX.X),
                      waits=waits, inc=True)
            t2 = S.op(s, lambda e: e.activation(
                out=sqjunk, in_=T_ap, func=AF.Square, accum_out=lnsc[:, 3:4]),
                waits=waits, inc=True)
            t1 = S.op(v, lambda e: e.tensor_scalar(
                out=lnsc[:, 0:1], in0=lnsc[:, 2:3], scalar1=INV_D, scalar2=None,
                op0=ALU.mult), waits=[t1, t2], inc=True)
            t1 = S.op(v, lambda e: e.tensor_tensor(
                out=lnsc[:, 4:5], in0=lnsc[:, 0:1], in1=lnsc[:, 0:1], op=ALU.mult),
                waits=[t1], inc=True)
            t1 = S.op(v, lambda e: e.scalar_tensor_tensor(
                out=lnsc[:, 5:6], in0=lnsc[:, 3:4], scalar=INV_D, in1=lnsc[:, 4:5],
                op0=ALU.mult, op1=ALU.subtract), waits=[t1], inc=True)
            t2 = S.op(s, lambda e: e.activation(
                out=lnsc[:, 6:7], in_=lnsc[:, 5:6], func=AF.Sqrt, bias=constt[:, 0:1]),
                waits=[t1], inc=True)
            t1 = S.op(v, lambda e: e.reciprocal(lnsc[:, 1:2], lnsc[:, 6:7]),
                      waits=[t2], inc=True)
            return t1

        def transpose_block(src_bf_ap, dst_col_fn, wait_src):
            evac_toks = []
            for kcq in range(KC // 4):
                bank = kcq % 2
                pso = PSAb[:, bank * 1024: bank * 1024 + 1024]
                for j in range(4):
                    kc = kcq * 4 + j
                    w = [wait_src]
                    if kcq >= 2:
                        w.append(evac_toks[kcq - 2])
                    S.op("tensor", lambda e, kc=kc, j=j, pso=pso: e.transpose(
                        pso[:, j * P:(j + 1) * P],
                        src_bf_ap[:, kc * P:(kc + 1) * P], identb[:]),
                        waits=w, inc=(j == 3))
                tt = S.tok("tensor")
                for j in range(4):
                    kc = kcq * 4 + j
                    dap, width, soff = dst_col_fn(kc)
                    S.op("scalar", lambda e, dap=dap, width=width, soff=soff, j=j,
                         pso=pso: e.activation(
                             out=dap, in_=pso[:, j * P + soff: j * P + soff + width],
                             func=AF.Copy), waits=[tt], inc=(j == 3))
                evac_toks.append(S.tok("scalar"))
            return evac_toks[-1]

        def ln_row_scalars(sum_ap, sq_ap, rb_slice, mb_slice, waits):
            v, s = "vector", "scalar"
            t1 = S.op(v, lambda e: e.tensor_scalar(
                out=r512[:, 0:512], in0=sum_ap, scalar1=INV_D, scalar2=None,
                op0=ALU.mult), waits=waits, inc=True)
            t1 = S.op(v, lambda e: e.tensor_tensor(
                out=r512[:, 1536:2048], in0=r512[:, 0:512], in1=r512[:, 0:512],
                op=ALU.mult), waits=[t1], inc=True)
            t1 = S.op(v, lambda e: e.scalar_tensor_tensor(
                out=r512[:, 1536:2048], in0=sq_ap, scalar=INV_D,
                in1=r512[:, 1536:2048], op0=ALU.mult, op1=ALU.subtract),
                waits=[t1], inc=True)
            t2 = S.op(s, lambda e: e.activation(
                out=r512[:, 1536:2048], in_=r512[:, 1536:2048], func=AF.Sqrt,
                bias=constt[0:1, 0:1]), waits=[t1], inc=True)
            t1 = S.op(v, lambda e: e.reciprocal(
                r512[:, 512:1024], r512[:, 1536:2048]), waits=[t2], inc=True)
            t1 = S.op(v, lambda e: e.tensor_tensor(
                out=r512[:, 1024:1536], in0=r512[:, 0:512], in1=r512[:, 512:1024],
                op=ALU.mult), waits=[t1], inc=True)
            t1 = S.op(v, lambda e: e.tensor_scalar(
                out=r512[:, 1024:1536], in0=r512[:, 1024:1536], scalar1=-1.0,
                scalar2=None, op0=ALU.mult), waits=[t1], inc=True)
            S.op("tensor", lambda e, rb=rb_slice: e.matmul(
                PSA[:, rb], onesr[:], r512[:, 512:1024], start=True, stop=True),
                waits=[t1])
            S.op("tensor", lambda e, mb=mb_slice: e.matmul(
                PSA[:, mb], onesr[:], r512[:, 1024:1536], start=True, stop=True),
                waits=[t1], inc=True)
            return S.tok("tensor")

        stopped = False
        for t in range(NT):
            if stopped:
                break
            tok0 = t * MT
            xrow0 = P + tok0
            sfx = "" if t == 0 else "@1"

            x_toks = []
            for b in range(NTB):
                x_toks.append(qX.dma(lambda e, b=b, xrow0=xrow0: e.dma_start(
                    out=x_blk[b], in_=x_in[xrow0 + b * P: xrow0 + (b + 1) * P, :]),
                    waits=zs_ready + [S.tok("vector"), S.tok("scalar"),
                                      S.tok("tensor")]))
            for b in range(NTB):
                stok = ln_stats_tm(x_blk[b], [x_toks[b], bias_tok, kconst])
                atok = S.op("vector", lambda e, b=b: e.tensor_scalar(
                    out=xln_scr, in0=x_blk[b], scalar1=lnsc[:, 0:1],
                    scalar2=lnsc[:, 1:2], op0=ALU.subtract, op1=ALU.mult),
                    waits=[stok], inc=True)
                tr_tok = transpose_block(
                    xln_scr,
                    lambda kc, b=b: (
                        actT[:, kc * ACTW + 1 + b * P: kc * ACTW + 1 + (b + 1) * P],
                        P, 0), atok)
            for side in range(2):
                if side == 0:
                    r0, r1 = xrow0 - P, xrow0
                    dstf = lambda kc: (actT[:, kc * ACTW: kc * ACTW + 1], 1, P - 1)
                else:
                    r0, r1 = xrow0 + MT, xrow0 + MT + P
                    dstf = lambda kc: (actT[:, kc * ACTW + 513: kc * ACTW + 514], 1, 0)
                htok = qX.dma(lambda e, r0=r0, r1=r1: e.dma_start(
                    out=xh_blk, in_=x_in[r0:r1, :]),
                    waits=[S.tok("scalar"), S.tok("vector")])
                stok = ln_stats_tm(xh_blk, [htok])
                atok = S.op("vector", lambda e: e.tensor_scalar(
                    out=xln_scr, in0=xh_blk, scalar1=lnsc[:, 0:1],
                    scalar2=lnsc[:, 1:2], op0=ALU.subtract, op1=ALU.mult),
                    waits=[stok], inc=True)
                tr_tok = transpose_block(xln_scr, dstf, atok)
            tr_all = tr_tok
            if stop == "actT" + sfx:
                emit_export("actT", actT, P, KC * ACTW, True, [tr_all])
                stopped = True
                break

            order = [(ncg, kc, w) for ncg in range(3) for kc in range(KC)
                     for w in range(3)]
            wtoks = {}
            mm_toks = {}

            def conv_dma(gi):
                ncg, kc, w = order[gi]
                buf = convbuf[:, (gi % 2) * 1024:(gi % 2) * 1024 + 1024]
                wwaits = [tr_all]
                if gi >= 2:
                    wwaits.append(mm_toks[gi - 2])
                wtoks[gi] = qW.dma(lambda e, kc=kc, w=w, buf=buf, ncg=ncg:
                                   e.dma_start(
                    out=buf, in_=convw[w, kc * P:(kc + 1) * P,
                                       ncg * 1024:(ncg + 1) * 1024]), waits=wwaits)

            conv_dma(0)
            conv_dma(1)
            qkv_tok = None
            for gi, (ncg, kc, w) in enumerate(order):
                buf = convbuf[:, (gi % 2) * 1024:(gi % 2) * 1024 + 1024]
                i_in_ncg = gi - ncg * 96
                ev_wait = S.tok("scalar")
                for nc8 in range(8):
                    S.op("tensor", lambda e, buf=buf, nc8=nc8, kc=kc, w=w,
                         i=i_in_ncg: e.matmul(
                             PSA[:, nc8 * 512:(nc8 + 1) * 512],
                             buf[:, nc8 * P:(nc8 + 1) * P],
                             actT[:, kc * ACTW + w: kc * ACTW + w + MT],
                             start=(i == 0), stop=(i == 95)),
                         waits=[wtoks[gi], ev_wait], inc=(nc8 == 7))
                mm_toks[gi] = S.tok("tensor")
                if gi + 2 < len(order):
                    conv_dma(gi + 2)
                if i_in_ncg == 95:
                    for nc8 in range(8):
                        ncx = ncg * 8 + nc8
                        qkv_tok = S.op("scalar", lambda e, nc8=nc8, ncx=ncx:
                                       e.activation(
                            out=qkv[ncx], in_=PSA[:, nc8 * 512:(nc8 + 1) * 512],
                            func=AF.Identity, bias=biases[:, ncx:ncx + 1]),
                            waits=[mm_toks[gi]], inc=True)
            qkv_done = qkv_tok
            if stop == "qkv" + sfx:
                emit_export("qkv", carve(AR2, 0, P, NCO * MT, BF16), P, NCO * MT,
                            True, [qkv_done])
                stopped = True
                break

            def load_wmat(src, waits, dst=None):
                dst = wmat if dst is None else dst
                tok = None
                for c in range(HEC):
                    tok = qW.dma(lambda e, src=src, c=c, dst=dst: e.dma_start(
                        out=dst[:, c * 1024:(c + 1) * 1024],
                        in_=src[c * P:(c + 1) * P, :]),
                        waits=(waits if c == 0 else ()))
                return tok

            def proj_heads(wtok, qkv_off, outbuf, bias_base, scale):
                done = []
                for hc in range(HEC):
                    bank = hc % 2
                    for kc in range(HEC):
                        S.op("tensor", lambda e, hc=hc, kc=kc, bank=bank: e.matmul(
                            PSA[:, bank * 512:(bank + 1) * 512],
                            wmat[:, kc * 1024 + hc * P: kc * 1024 + (hc + 1) * P],
                            qkv[qkv_off + kc], start=(kc == 0), stop=(kc == HEC - 1)),
                            waits=[wtok] + ([done[hc - 2]] if hc >= 2 else []),
                            inc=(kc == HEC - 1))
                    mt_ = S.tok("tensor")
                    done.append(S.op("scalar", lambda e, hc=hc, bank=bank:
                                     e.activation(
                        out=outbuf[hc], in_=PSA[:, bank * 512:(bank + 1) * 512],
                        func=AF.Identity, bias=biases[:, bias_base + hc:bias_base + hc + 1],
                        scale=scale), waits=[mt_], inc=True))
                return done[-1]

            wq_tok = load_wmat(wq_in, [qkv_done])
            qh_tok = proj_heads(wq_tok, 0, qh_s, 24, constt[:, 1:2])
            wk_tok = load_wmat(wk_in, [qh_tok])
            kh_tok = proj_heads(wk_tok, 8, kh_s, 32, 1.0)
            wv_tok = load_wmat(wv_in, [kh_tok])
            bv_tok = qW.dma(lambda e: e.dma_start(out=bvrow, in_=bv_in),
                            waits=[kh_tok])
            for i2 in range(2):
                S.op("tensor", lambda e, i2=i2: e.matmul(
                    PSA[:, 1024 + i2 * 512: 1024 + (i2 + 1) * 512],
                    onesr[:], bvrow[:, i2 * 512:(i2 + 1) * 512],
                    start=True, stop=True), waits=[bv_tok, kh_tok], inc=(i2 == 1))
            bvb_t = S.tok("tensor")
            bv_ev = S.op("vector", lambda e: e.tensor_copy(
                out=bv_bc, in_=PSA[:, 1024:2048]), waits=[bvb_t], inc=True)
            vh_done = []
            for tb in range(NTB):
                for half in range(2):
                    bank = half
                    for kc in range(HEC):
                        S.op("tensor", lambda e, tb=tb, half=half, kc=kc, bank=bank:
                             e.matmul(
                                 PSA[:, bank * 512:(bank + 1) * 512],
                                 qkv[16 + kc][:, tb * P:(tb + 1) * P],
                                 wmat[:, kc * 1024 + half * 512:
                                      kc * 1024 + (half + 1) * 512],
                                 start=(kc == 0), stop=(kc == HEC - 1)),
                             waits=[wv_tok, bv_ev] +
                                   ([vh_done[-2]] if len(vh_done) >= 2 else []),
                             inc=(kc == HEC - 1))
                    mt_ = S.tok("tensor")
                    vh_done.append(S.op("vector",
                        lambda e, tb=tb, half=half, bank=bank: e.tensor_tensor(
                            out=vh_tm[tb][:, half * 512:(half + 1) * 512],
                            in0=PSA[:, bank * 512:(bank + 1) * 512],
                            in1=bv_bc[:, half * 512:(half + 1) * 512], op=ALU.add),
                        waits=[mt_], inc=True))
            vh_all = vh_done[-1]
            if stop == "vh" + sfx:
                emit_export("qh", carve(AR2, 24576, P, HEC * MT, BF16), P,
                            HEC * MT, True, [vh_all])
                emit_export("kh", carve(AR2, 32768, P, HEC * MT, BF16), P,
                            HEC * MT, True, [vh_all])
                emit_export("vh", carve(AR2, 40960, P, NTB * W_, BF16), P,
                            NTB * W_, True, [vh_all])
                stopped = True
                break

            od_tok = None
            for hc in range(HEC):
                od_tok = qX.dma(lambda e, hc=hc: e.dma_start(
                    out=qh_od[hc][:], in_=qh_s[hc][64:128, :]),
                    waits=[qh_tok, vh_all])
                od_tok = qX.dma(lambda e, hc=hc: e.dma_start(
                    out=kh_od[hc][:], in_=kh_s[hc][64:128, :]),
                    waits=[kh_tok, vh_all])
            S.op(g, lambda e: e.memset(mask16, 0.0), waits=[vh_all, S.tok("tensor")])
            mk = S.op(g, lambda e: e.affine_select(
                out=mask16, in_=mask16, compare_op=ALU.is_ge, fill=NEG,
                base=0, pattern=[[0, H], [-1, P]], channel_multiplier=1), inc=True)
            attn_last = None
            mask_done = None
            pt_done = None
            exp_done = None
            trp_done = None
            for tb in range(NTB):
                for hh in range(H):
                    hc, hp = hh // 2, (hh % 2) * 64
                    qsl = (qh_s[hc][0:64, tb * P:(tb + 1) * P] if hp == 0
                           else qh_od[hc][:, tb * P:(tb + 1) * P])
                    ksl = (kh_s[hc][0:64, tb * P:(tb + 1) * P] if hp == 0
                           else kh_od[hc][:, tb * P:(tb + 1) * P])
                    S.op("tensor", lambda e, hh=hh, qsl=qsl, ksl=ksl: e.matmul(
                        PSA[:, hh * P:(hh + 1) * P], qsl, ksl,
                        start=True, stop=True, skip_group_check=True),
                        waits=[mk, vh_all, od_tok] +
                              ([mask_done] if mask_done else []),
                        inc=(hh == H - 1))
                sc_t = S.tok("tensor")
                v_t = S.op("vector", lambda e: e.tensor_tensor(
                    out=attn_sc_v, in0=PSA[:, 0:2048], in1=mask16, op=ALU.add),
                    waits=[sc_t] + ([exp_done] if exp_done else []), inc=True)
                mask_done = v_t
                v_t = S.op("vector", lambda e: e.tensor_reduce(
                    out=attn_mx[:, 0:H],
                    in_=attn_sc_v.rearrange("p (h k) -> p h k", h=H),
                    axis=AX.X, op=ALU.max, negate=True), waits=[v_t], inc=True)
                for hh in range(H):
                    S.op("scalar", lambda e, hh=hh: e.activation(
                        out=attn_e_v[:, hh * P:(hh + 1) * P],
                        in_=attn_sc_v[:, hh * P:(hh + 1) * P],
                        func=AF.Exp, bias=attn_mx[:, hh:hh + 1]),
                        waits=[v_t], inc=(hh == H - 1))
                s_t = S.tok("scalar")
                exp_done = s_t
                v_t = S.op("vector", lambda e: e.tensor_reduce(
                    out=attn_mx[:, H:2 * H],
                    in_=attn_e_v.rearrange("p (h k) -> p h k", h=H),
                    axis=AX.X, op=ALU.add), waits=[s_t], inc=True)
                v_t = S.op("vector", lambda e: e.reciprocal(
                    attn_mx[:, 2 * H:3 * H], attn_mx[:, H:2 * H]),
                    waits=[v_t], inc=True)
                for hh in range(H):
                    S.op("vector", lambda e, hh=hh: e.tensor_scalar(
                        out=attn_p_v[:, hh * P:(hh + 1) * P],
                        in0=attn_e_v[:, hh * P:(hh + 1) * P],
                        scalar1=attn_mx[:, 2 * H + hh:2 * H + hh + 1], scalar2=None,
                        op0=ALU.mult),
                        waits=[v_t] + ([trp_done] if trp_done else []),
                        inc=(hh == H - 1))
                v_t = S.tok("vector")
                for hh in range(H):
                    S.op("tensor", lambda e, hh=hh: e.transpose(
                        PSAb[:, 4096 + hh * P: 4096 + (hh + 1) * P],
                        attn_p_v[:, hh * P:(hh + 1) * P], identb[:]),
                        waits=[v_t] + ([pt_done] if pt_done else []),
                        inc=(hh == H - 1))
                t_t = S.tok("tensor")
                trp_done = t_t
                s_t = S.op("scalar", lambda e: e.activation(
                    out=attn_pT_v, in_=PSAb[:, 4096:6144], func=AF.Copy),
                    waits=[t_t], inc=True)
                pt_done = s_t
                for hh in range(H):
                    hc, hp = hh // 2, (hh % 2) * 64
                    S.op("tensor", lambda e, hh=hh, hc=hc, hp=hp, tb=tb: e.matmul(
                        PSA[hp:hp + 64, 3072 + hc * P: 3072 + (hc + 1) * P],
                        vh_tm[tb][:, hh * 64:(hh + 1) * 64],
                        attn_pT_v[:, hh * P:(hh + 1) * P],
                        start=True, stop=True, skip_group_check=True),
                        waits=[s_t], inc=(hh == H - 1))
                t_t = S.tok("tensor")
                attn_last = S.op("scalar", lambda e, tb=tb: e.activation(
                    out=o_fm_all.rearrange("p (c n) -> p c n", c=HEC)[
                        :, :, tb * P:(tb + 1) * P],
                    in_=PSA[:, 3072:4096].rearrange("p (c n) -> p c n", c=HEC),
                    func=AF.Copy), waits=[t_t], inc=True)
                if stop == "attn1" + sfx and tb == 0:
                    break
            if stop == "attn1" + sfx:
                emit_export("qod", carve(AR2, 81920, 64, HEC * MT, BF16), 64,
                            HEC * MT, True, [attn_last])
                emit_export("sc", attn_sc_v, P, H * P, False, [attn_last])
                emit_export("mx", attn_mx[:], P, 3 * H, False, [attn_last])
                emit_export("pv", attn_p_v, P, H * P, True, [attn_last])
                emit_export("pT", attn_pT_v, P, H * P, True, [attn_last])
                emit_export("o1", o_fm_all, P, HEC * MT, True, [attn_last])
                stopped = True
                break
            if stop == "o_fm" + sfx:
                emit_export("o_fm", o_fm_all, P, HEC * MT, True, [attn_last])
                stopped = True
                break

            wo_tok = load_wmat(wo_in, [attn_last], dst=wmat2)
            ao = ao_buf
            ao_done = []
            for wc in range(HEC):
                bank = wc % 2
                for hc in range(HEC):
                    S.op("tensor", lambda e, wc=wc, hc=hc, bank=bank: e.matmul(
                        PSA[:, bank * 512:(bank + 1) * 512],
                        wmat2[:, hc * 1024 + wc * P: hc * 1024 + (wc + 1) * P],
                        o_fm[hc], start=(hc == 0), stop=(hc == HEC - 1)),
                        waits=[wo_tok] + ([ao_done[wc - 2]] if wc >= 2 else []),
                        inc=(hc == HEC - 1))
                mt_ = S.tok("tensor")
                ao_done.append(S.op("scalar", lambda e, wc=wc, bank=bank:
                                    e.activation(
                    out=ao[wc], in_=PSA[:, bank * 512:(bank + 1) * 512],
                    func=AF.Identity, bias=biases[:, 40 + wc:40 + wc + 1]),
                    waits=[mt_], inc=True))
            ao_all = ao_done[-1]
            if stop == "ao" + sfx:
                emit_export("ao", carve(AR2, 69632, P, HEC * MT, BF16), P,
                            HEC * MT, True, [ao_all])
                stopped = True
                break

            pj_toks = {}
            at_done = []

            def pj_dma(nc_i, waits):
                buf = pjw_b[nc_i % 2]
                tok = None
                for c in range(HEC):
                    tok = qW.dma(lambda e, nc_i=nc_i, buf=buf, c=c: e.dma_start(
                        out=buf[:, c * P:(c + 1) * P],
                        in_=projw[c * P:(c + 1) * P, nc_i * P:(nc_i + 1) * P]),
                        waits=(waits if c == 0 else ()))
                pj_toks[nc_i] = tok

            pj_dma(0, [ao_all])
            pj_dma(1, [ao_all])
            for nc_i in range(KC):
                buf = pjw_b[nc_i % 2]
                bank = nc_i % 2
                for kc in range(HEC):
                    S.op("tensor", lambda e, nc_i=nc_i, kc=kc, buf=buf,
                         bank=bank: e.matmul(
                             PSA[:, bank * 512:(bank + 1) * 512],
                             buf[:, kc * P:(kc + 1) * P],
                             ao[kc],
                             start=(kc == 0), stop=(kc == HEC - 1)),
                         waits=[pj_toks[nc_i]] +
                               ([at_done[nc_i - 2]] if nc_i >= 2 else []),
                         inc=(kc == HEC - 1))
                mt_ = S.tok("tensor")
                if nc_i + 2 < KC:
                    pj_dma(nc_i + 2, [mt_])
                stg = af_buf[nc_i]
                ev = S.op("scalar", lambda e, nc_i=nc_i, bank=bank, stg=stg:
                          e.activation(
                              out=stg, in_=PSA[:, bank * 512:(bank + 1) * 512],
                              func=AF.Identity, bias=biases[:, 48 + nc_i:48 + nc_i + 1]),
                          waits=[mt_], inc=True)
                cp = S.op("scalar", lambda e, stg=stg: e.activation(
                    out=p7_cp, in_=stg, func=AF.Copy), waits=[ev], inc=True)
                sq = S.op("scalar", lambda e, stg=stg: e.activation(
                    out=p7_sq, in_=stg, func=AF.Square), waits=[cp], inc=True)
                S.op("tensor", lambda e, nc_i=nc_i: e.matmul(
                    PSA[0:1, 1024:1536], onescol_b[:], p7_cp[:],
                    start=(nc_i == 0), stop=(nc_i == KC - 1), skip_group_check=True),
                    waits=[cp])
                S.op("tensor", lambda e, nc_i=nc_i: e.matmul(
                    PSA[0:1, 1536:2048], onescol_b[:], p7_sq[:],
                    start=(nc_i == 0), stop=(nc_i == KC - 1), skip_group_check=True),
                    waits=[sq], inc=True)
                at_done.append(sq)
            if stop == "attn_sp" + sfx:
                for nc_i in range(KC):
                    qY.dma(lambda e, nc_i=nc_i: e.dma_start(
                        out=at_sp[nc_i * P:(nc_i + 1) * P, :], in_=af_buf[nc_i]),
                        waits=[at_done[-1], S.tok("tensor")])
                S.emit("sync", lambda e: e.nop(), waits=qY.final_waits())
                stopped = True
                break

            bc2 = ln_row_scalars(PSA[0:1, 1024:1536], PSA[0:1, 1536:2048],
                                 BK(6), BK(7),
                                 [S.tok("tensor"), at_done[-1]])
            y1_done = []
            for nc_i in range(KC):
                for tb in range(NTB):
                    S.op("tensor", lambda e, tb=tb, nc_i=nc_i: e.transpose(
                        PSA[:, 2048 + tb * P: 2048 + (tb + 1) * P],
                        x_blk[tb][:, nc_i * P:(nc_i + 1) * P], identf[:]),
                        waits=[bc2] + ([y1_done[nc_i - 2][0]] if nc_i >= 2 else []),
                        inc=(tb == NTB - 1))
                tx = S.tok("tensor")
                evx = S.op("vector", lambda e: e.tensor_copy(
                    out=xfm_st, in_=PSA[:, 2048:2560]), waits=[tx], inc=True)
                v1 = S.op("vector", lambda e, nc_i=nc_i: e.tensor_tensor(
                    out=y1_st, in0=af_buf[nc_i], in1=PSA[:, 3072:3584],
                    op=ALU.mult), waits=[bc2], inc=True)
                v1 = S.op("vector", lambda e: e.tensor_tensor(
                    out=y1_st, in0=y1_st, in1=PSA[:, 3584:4096], op=ALU.add),
                    waits=[v1], inc=True)
                v1 = S.op("vector", lambda e, nc_i=nc_i: e.tensor_tensor(
                    out=af_buf[nc_i], in0=y1_st, in1=xfm_st, op=ALU.add),
                    waits=[v1, evx], inc=True)
                cp = S.op("scalar", lambda e, nc_i=nc_i: e.activation(
                    out=p7_cp, in_=af_buf[nc_i], func=AF.Copy), waits=[v1], inc=True)
                sq = S.op("scalar", lambda e, nc_i=nc_i: e.activation(
                    out=p7_sq, in_=af_buf[nc_i], func=AF.Square),
                    waits=[cp], inc=True)
                S.op("tensor", lambda e, nc_i=nc_i: e.matmul(
                    PSA[0:1, 1024:1536], onescol_b[:], p7_cp[:],
                    start=(nc_i == 0), stop=(nc_i == KC - 1), skip_group_check=True),
                    waits=[cp])
                S.op("tensor", lambda e, nc_i=nc_i: e.matmul(
                    PSA[0:1, 1536:2048], onescol_b[:], p7_sq[:],
                    start=(nc_i == 0), stop=(nc_i == KC - 1), skip_group_check=True),
                    waits=[sq], inc=True)
                sp = qY.dma(lambda e, nc_i=nc_i, t=t: e.dma_start(
                    out=y1_sp[t, nc_i * P:(nc_i + 1) * P, :], in_=af_buf[nc_i]),
                    waits=[v1, S.tok("tensor")])
                y1_done.append((v1, sp))
            if stop == "y1" + sfx:
                S.emit("sync", lambda e: e.nop(), waits=[y1_done[-1][1]])
                stopped = True
                break

            bc3 = ln_row_scalars(PSA[0:1, 1024:1536], PSA[0:1, 1536:2048],
                                 BK(6), BK(7),
                                 [S.tok("tensor"), y1_done[-1][0]])
            x2_done = []
            for nc_i in range(KC):
                v1 = S.op("vector", lambda e, nc_i=nc_i: e.tensor_tensor(
                    out=y1_st, in0=af_buf[nc_i], in1=PSA[:, 3072:3584], op=ALU.mult),
                    waits=[bc3], inc=True)
                x2_done.append(S.op("vector", lambda e, nc_i=nc_i: e.tensor_tensor(
                    out=actT[:, nc_i * ACTW: nc_i * ACTW + MT], in0=y1_st,
                    in1=PSA[:, 3584:4096], op=ALU.add), waits=[v1], inc=True))
            x2_all = x2_done[-1]
            if stop == "xln2T" + sfx:
                emit_export("xln2T", actT, P, KC * ACTW, True, [x2_all])
                stopped = True
                break

            for kc in range(KC):
                S.op("tensor", lambda e, kc=kc: e.matmul(
                    PSA[0:KC, 0:512], c32[:, 32 - kc:64 - kc],
                    actT[:, kc * ACTW: kc * ACTW + MT],
                    start=(kc == 0), stop=(kc == KC - 1)),
                    waits=[x2_all], inc=(kc == KC - 1))
            xg_t = S.tok("tensor")
            xg_tok = S.op("scalar", lambda e: e.activation(
                out=xg_sb[:], in_=PSA[0:KC, 0:512], func=AF.Copy),
                waits=[xg_t], inc=True)

            S.op(g, lambda e: e.memset(sel32, 1.0), waits=[x2_all])
            S.op(g, lambda e: e.affine_select(
                out=sel32, in_=sel32, compare_op=ALU.is_ge, fill=0.0,
                base=0, pattern=[[1, KC * P]], channel_multiplier=-P))
            sel_t = S.op(g, lambda e: e.affine_select(
                out=sel32, in_=sel32, compare_op=ALU.is_ge, fill=0.0,
                base=127, pattern=[[-1, KC * P]], channel_multiplier=P), inc=True)

            y1_spilled = qY.final_waits()
            win_free = {4: None, 5: None}
            prev_ext = {0: None, 1: None}
            h_done = []
            for ns in range(FC):
                pair_tok = {}
                for mi, (bsrc, ssrc, zsrc) in enumerate(
                        ((ub_in, us_in, zsu_d), (gb_in, gs_in, zsg_d))):
                    bi = mi
                    bw = [sel_t, xg_tok, S.tok("vector")] + y1_spilled
                    if prev_ext[bi] is not None:
                        bw.append(prev_ext[bi])
                    bt = qB.dma(lambda e, bsrc=bsrc, ns=ns, bi=bi: e.dma_start(
                        out=gu_byte[bi], in_=bsrc[ns]), waits=bw)
                    st_ = qW.dma(lambda e, ssrc=ssrc, ns=ns, mi=mi: e.dma_start(
                        out=ssl[mi], in_=ssrc[:, ns * P:(ns + 1) * P]),
                        waits=[S.tok("tensor")])
                    zt_ = qB.dma(lambda e, zsrc=zsrc, ns=ns, mi=mi: e.dma_start(
                        out=zsl[mi], in_=zsrc[:, ns * P:(ns + 1) * P]),
                        waits=[S.tok("tensor")])
                    et = ext4("vector", gu_byte[bi], gu_nib[bi], KC * 16,
                              [bt, S.tok("vector")])
                    prev_ext[bi] = et
                    win = 4 + mi
                    for q8 in range(8):
                        for j4 in range(4):
                            kc = q8 * 4 + j4
                            bcw = [st_, sel_t]
                            if win_free[win] is not None:
                                bcw.append(win_free[win])
                            S.op("tensor", lambda e, kc=kc, j4=j4, win=win, mi=mi:
                                 e.matmul(
                                     PSA[:, win * 512 + j4 * P:
                                         win * 512 + (j4 + 1) * P],
                                     sel32[:, kc * P:(kc + 1) * P], ssl[mi][:],
                                     start=True, stop=True, skip_group_check=True),
                                 waits=bcw, inc=(j4 == 3))
                        bc_t = S.tok("tensor")
                        mtok = S.op("vector", lambda e, q8=q8, bi=bi, win=win:
                                    e.tensor_tensor(
                                out=gu_w[bi][:, q8 * 512:(q8 + 1) * 512],
                                in0=gu_nib[bi][:, q8 * 512:(q8 + 1) * 512],
                                in1=PSA[:, win * 512:(win + 1) * 512], op=ALU.mult),
                                waits=[et, bc_t], inc=True)
                        win_free[win] = mtok
                    wq_done = S.tok("vector")
                    accb = mi
                    S.op("tensor", lambda e, mi=mi, accb=accb: e.matmul(
                        PSA[:, accb * 512:(accb + 1) * 512],
                        zsl[mi][:], xg_sb[:],
                        start=True, stop=False, skip_group_check=True),
                        waits=[zt_, xg_tok, wq_done, S.tok("scalar")])
                    for kc in range(KC):
                        S.op("tensor", lambda e, kc=kc, bi=bi, accb=accb: e.matmul(
                            PSA[:, accb * 512:(accb + 1) * 512],
                            gu_w[bi][:, kc * P:(kc + 1) * P],
                            actT[:, kc * ACTW: kc * ACTW + MT],
                            start=False, stop=(kc == KC - 1),
                            skip_group_check=True),
                            waits=[wq_done], inc=(kc == KC - 1))
                    pair_tok[mi] = S.tok("tensor")
                sl = S.op("scalar", lambda e: e.activation(
                    out=silu_sb, in_=PSA[:, 512:1024], func=AF.Silu),
                    waits=[pair_tok[1]], inc=True)
                h_done.append(S.op("vector", lambda e, ns=ns: e.tensor_tensor(
                    out=h_fm[ns], in0=silu_sb, in1=PSA[:, 0:512], op=ALU.mult),
                    waits=[sl, pair_tok[0]], inc=True))
            h_all = h_done[-1]
            if stop == "h" + sfx:
                emit_export("h", carve(AR2, 0, P, FC * MT, BF16), P, FC * MT,
                            True, [h_all])
                stopped = True
                break

            for kc in range(FC):
                S.op("tensor", lambda e, kc=kc: e.matmul(
                    PSA[0:FC, 0:512], c86[:, 86 - kc:172 - kc], h_fm[kc],
                    start=(kc == 0), stop=(kc == FC - 1)),
                    waits=[h_all], inc=(kc == FC - 1))
            hg_t = S.tok("tensor")
            hg_tok = S.op("scalar", lambda e: e.activation(
                out=hg_sb[:], in_=PSA[0:FC, 0:512], func=AF.Copy),
                waits=[hg_t], inc=True)

            win_free = {4: None, 5: None}
            prev_extd = {0: None, 1: None}
            ffn_done = []
            for ncg in range(8):
                first_sc = S.tok("scalar")
                for kc in range(FC):
                    bi = kc % 2
                    bw = [hg_tok, S.tok("vector")]
                    if prev_extd[bi] is not None:
                        bw.append(prev_extd[bi])
                    bt = qB.dma(lambda e, ncg=ncg, kc=kc, bi=bi: e.dma_start(
                        out=dn_byte[bi][:, 0:256], in_=db_in[ncg, kc]), waits=bw)
                    st_ = qW.dma(lambda e, ncg=ncg, kc=kc, bi=bi: e.dma_start(
                        out=dsl[bi], in_=ds_in[kc:kc + 1,
                                              ncg * 512:(ncg + 1) * 512]),
                        waits=[S.tok("tensor")])
                    et = ext4("vector", dn_byte[bi][:, 0:256], dn_nib[bi], 64,
                              [bt, S.tok("vector")])
                    prev_extd[bi] = et
                    win = 4 + bi
                    bcw = [st_]
                    if win_free[win] is not None:
                        bcw.append(win_free[win])
                    S.op("tensor", lambda e, bi=bi, win=win: e.matmul(
                        PSA[:, win * 512:(win + 1) * 512], onesr_b[:], dsl[bi][:],
                        start=True, stop=True, skip_group_check=True),
                        waits=bcw, inc=True)
                    bc_t = S.tok("tensor")
                    wd_done = S.op("vector", lambda e, bi=bi, win=win:
                                   e.tensor_tensor(
                        out=dn_w[bi][:], in0=dn_nib[bi][:],
                        in1=PSA[:, win * 512:(win + 1) * 512], op=ALU.mult),
                        waits=[et, bc_t], inc=True)
                    win_free[win] = wd_done
                    for nc4 in range(4):
                        S.op("tensor", lambda e, kc=kc, nc4=nc4, bi=bi: e.matmul(
                            PSA[:, nc4 * 512:(nc4 + 1) * 512],
                            dn_w[bi][:, nc4 * P:(nc4 + 1) * P], h_fm[kc],
                            start=(kc == 0), stop=False, skip_group_check=True),
                            waits=[wd_done, hg_tok, first_sc], inc=(nc4 == 3))
                for nc4 in range(4):
                    nc_i = ncg * 4 + nc4
                    zt_ = qB.dma(lambda e, nc_i=nc_i, nc4=nc4: e.dma_start(
                        out=nzd[nc4 % 2], in_=zsd_d[:, nc_i * P:(nc_i + 1) * P]),
                        waits=[S.tok("tensor")])
                    S.op("tensor", lambda e, nc4=nc4: e.matmul(
                        PSA[:, nc4 * 512:(nc4 + 1) * 512],
                        nzd[nc4 % 2][:], hg_sb[:],
                        start=False, stop=True, skip_group_check=True),
                        waits=[zt_, hg_tok], inc=True)
                    mt_ = S.tok("tensor")
                    stg = ff_buf[nc_i]
                    ev = S.op("scalar", lambda e, nc4=nc4, stg=stg: e.activation(
                        out=stg, in_=PSA[:, nc4 * 512:(nc4 + 1) * 512],
                        func=AF.Copy), waits=[mt_], inc=True)
                    cp = S.op("scalar", lambda e, stg=stg: e.activation(
                        out=st_cp, in_=stg, func=AF.Copy), waits=[ev], inc=True)
                    sq = S.op("scalar", lambda e, stg=stg: e.activation(
                        out=st_sq, in_=stg, func=AF.Square), waits=[cp], inc=True)
                    S.op("tensor", lambda e, nc_i=nc_i: e.matmul(
                        PSA[0:1, 3072:3584], onescol_b[:], st_cp[:],
                        start=(nc_i == 0), stop=(nc_i == KC - 1),
                        skip_group_check=True), waits=[cp])
                    S.op("tensor", lambda e, nc_i=nc_i: e.matmul(
                        PSA[0:1, 3584:4096], onescol_b[:], st_sq[:],
                        start=(nc_i == 0), stop=(nc_i == KC - 1),
                        skip_group_check=True), waits=[sq], inc=True)
                    ffn_done.append(sq)

            if stop == "ffn_sp" + sfx:
                emit_export("hg", hg_sb[:], FC, MT, True, [ffn_done[-1]],
                            tmp_off=58368)
                for nc_i in range(KC):
                    qY.dma(lambda e, nc_i=nc_i: e.dma_start(
                        out=at_sp[nc_i * P:(nc_i + 1) * P, :], in_=ff_buf[nc_i]),
                        waits=[ffn_done[-1], S.tok("tensor")])
                S.emit("sync", lambda e: e.nop(), waits=qY.final_waits())
                stopped = True
                break

            bc4 = ln_row_scalars(PSA[0:1, 3072:3584], PSA[0:1, 3584:4096],
                                 BK(4), BK(5),
                                 [S.tok("tensor"), ffn_done[-1]])
            out_toks = []
            y_rts = []
            for nc_i in range(KC):
                ow = [bc4] + ([out_toks[nc_i - 2]] if nc_i >= 2 else [])
                y_rts.append(qY.dma(lambda e, nc_i=nc_i, t=t: e.dma_start(
                    out=y1_ld[nc_i % 2], in_=y1_sp[t, nc_i * P:(nc_i + 1) * P, :]),
                    waits=[bc4] + ([S.tok("vector")] if nc_i >= 2 else [])))
                v1 = S.op("vector", lambda e, nc_i=nc_i: e.tensor_tensor(
                    out=y2_st[nc_i % 2], in0=ff_buf[nc_i], in1=PSA[:, 2048:2560],
                    op=ALU.mult), waits=ow, inc=True)
                v1 = S.op("vector", lambda e, nc_i=nc_i: e.tensor_tensor(
                    out=y2_st[nc_i % 2], in0=y2_st[nc_i % 2], in1=PSA[:, 2560:3072],
                    op=ALU.add), waits=[v1], inc=True)
                v1 = S.op("vector", lambda e, nc_i=nc_i: e.tensor_tensor(
                    out=y2_st[nc_i % 2], in0=y2_st[nc_i % 2], in1=y1_ld[nc_i % 2],
                    op=ALU.add), waits=[v1, y_rts[nc_i]], inc=True)
                out_toks.append(qX.dma(
                    lambda e, nc_i=nc_i, tok0=tok0: e.dma_start(
                        out=out_d[nc_i * P:(nc_i + 1) * P, tok0:tok0 + MT],
                        in_=y2_st[nc_i % 2]), waits=[v1]))
            if stop == "tile0" + sfx:
                stopped = True
                break

        fw = qX.final_waits() + qY.final_waits() + qB.final_waits() + qW.final_waits()
        S.emit("sync", lambda e: e.nop(), waits=fw)

        block = st.enter_context(nc.Block())

        def mk_runner(eng):
            items = S.q[eng]

            def run(e):
                for fn, w, ii in items:
                    for sh, v in w:
                        e.wait_ge(sh, v)
                    ins = fn(e)
                    for sh, k in ii:
                        ins.then_inc(sh, k)
            return run

        for eng in ENGINES:
            getattr(block, eng)(mk_runner(eng))
    return nc, dbg



def _prep_core_inputs(core, inp):
    x = np.ascontiguousarray(inp["L0_attn_inputs"].reshape(NCORES * TPC, D))
    g0 = core * TPC
    seq = g0 // SEQ
    s0, s1 = seq * SEQ, (seq + 1) * SEQ
    xc = np.zeros((TPC + 2 * P, D), np.float32)
    lo, hi = max(g0 - P, s0), min(g0 + TPC + P, s1)
    xc[lo - (g0 - P): hi - (g0 - P)] = x[lo:hi]

    u8 = lambda a: np.ascontiguousarray(a).view(np.uint8)
    gb = u8(inp["L0_ffn_gate_qweight"]).reshape(KC, P, FC, 64)
    gb = np.ascontiguousarray(gb.transpose(2, 1, 0, 3).reshape(FC, P, KC * 64))
    ub = u8(inp["L0_ffn_up_qweight"]).reshape(KC, P, FC, 64)
    ub = np.ascontiguousarray(ub.transpose(2, 1, 0, 3).reshape(FC, P, KC * 64))
    dbv = u8(inp["L0_ffn_down_qweight"]).reshape(FC, P, 8, 256)
    dbv = np.ascontiguousarray(dbv.transpose(2, 0, 1, 3))

    m = dict(
        x_in=xc,
        convw=np.ascontiguousarray(inp["L0_attn_conv_w"]),
        convb=np.ascontiguousarray(inp["L0_attn_conv_b"].reshape(CONVO, 1)),
        wq_in=np.ascontiguousarray(inp["L0_attn_wq"].reshape(W_, W_)),
        wk_in=np.ascontiguousarray(inp["L0_attn_wk"].reshape(W_, W_)),
        wv_in=np.ascontiguousarray(inp["L0_attn_wv"].reshape(W_, W_)),
        wo_in=np.ascontiguousarray(inp["L0_attn_wo"].reshape(W_, W_)),
        bq_in=np.ascontiguousarray(inp["L0_attn_bq"].reshape(W_, 1)),
        bk_in=np.ascontiguousarray(inp["L0_attn_bk"].reshape(W_, 1)),
        bv_in=np.ascontiguousarray(inp["L0_attn_bv"].reshape(1, W_)),
        bo_in=np.ascontiguousarray(inp["L0_attn_bo"].reshape(W_, 1)),
        projw=np.ascontiguousarray(inp["L0_attn_proj_w"]),
        projb=np.ascontiguousarray(inp["L0_attn_proj_b"].reshape(KC, P).T),
        gb_in=gb, ub_in=ub, db_in=dbv,
        gs_in=np.ascontiguousarray(inp["L0_ffn_gate_scales"]),
        us_in=np.ascontiguousarray(inp["L0_ffn_up_scales"]),
        ds_in=np.ascontiguousarray(inp["L0_ffn_down_scales"]),
        gz_in=u8(inp["L0_ffn_gate_qzeros"]).reshape(KC, DFF // 2),
        uz_in=u8(inp["L0_ffn_up_qzeros"]).reshape(KC, DFF // 2),
        dz_in=u8(inp["L0_ffn_down_qzeros"]).reshape(FC, D // 2),
    )
    return m


def kernel(**inputs):
    nc, _ = build_core()
    in_maps = [_prep_core_inputs(c, inputs) for c in range(NCORES)]
    res = run_bass_kernel_spmd(nc, in_maps, core_ids=list(range(NCORES)))
    outs = [np.ascontiguousarray(r["out_d"].T) for r in res.results]
    return np.concatenate(outs, axis=0).reshape(4, SEQ, D)


# version 03 (final)
# speedup vs baseline: 1.0353x; 1.0051x over previous
import numpy as np
import concourse.bass as bass
import concourse.mybir as mybir
from concourse.bass_utils import run_bass_kernel_spmd
from contextlib import ExitStack

F32 = mybir.dt.float32
BF16 = mybir.dt.bfloat16
U8 = mybir.dt.uint8
AF = mybir.ActivationFunctionType
ALU = mybir.AluOpType
AX = mybir.AxisListType

P = 128
D = 4096
KC = D // P
CONVO = 3072
NCO = CONVO // P
W_ = 1024
HEC = W_ // P
H, DH = 16, 64
DFF = 11008
FC = DFF // P
MT = 512
NTB = MT // P
NT = 2
TPC = NT * MT
NCORES = 8
SEQ = 2048
NEG = -1.0e9
EPS = 1e-5
INV_D = 1.0 / 4096.0

ENGINES = ("tensor", "vector", "scalar", "gpsimd", "sync")

def BK(i, w=512):
    return slice(i * 512, i * 512 + w)


class Sched:
    def __init__(self, nc, st):
        self.nc = nc
        self.st = st
        self.q = {e: [] for e in ENGINES}
        self.sems = {}
        self.cnt = {}
        self.last_wait = {e: {} for e in ENGINES}

    def new_sem(self, name):
        h = self.st.enter_context(self.nc.semaphore(name))
        self.sems[name] = h
        self.cnt[name] = 0
        return name

    @staticmethod
    def _flat(waits):
        out = []
        for w in waits:
            if isinstance(w, tuple):
                out.append(w)
            else:
                out.extend(w)
        return out

    def emit(self, eng, fn, waits=(), incs=()):
        w = []
        for sname, val in self._flat(waits):
            if val <= 0:
                continue
            if self.last_wait[eng].get(sname, -1) >= val:
                continue
            self.last_wait[eng][sname] = val
            w.append((self.sems[sname], val))
        ii = []
        for sname, k in incs:
            self.cnt[sname] += k
            ii.append((self.sems[sname], k))
        self.q[eng].append((fn, w, ii))

    def op(self, eng, fn, waits=(), inc=False):
        sname = "e_" + eng
        self.emit(eng, fn, waits=waits, incs=([(sname, 1)] if inc else ()))
        return (sname, self.cnt[sname])

    def tok(self, eng):
        return ("e_" + eng, self.cnt["e_" + eng])


class DmaQ:
    def __init__(self, S, eng, name):
        self.S = S
        self.eng = eng
        self.sem = [S.new_sem(name + "0"), S.new_sem(name + "1")]
        self.i = 0

    def dma(self, fn, waits=()):
        par = self.i % 2
        self.i += 1
        sname = self.sem[par]
        w = list(waits) + [(sname, self.S.cnt[sname])]
        self.S.emit(self.eng, fn, waits=w, incs=[(sname, 16)])
        return [(s, self.S.cnt[s]) for s in self.sem]

    def final_waits(self):
        return [(s, self.S.cnt[s]) for s in self.sem]


def build_core(stop=None):
    nc = bass.Bass("TRN2", target_bir_lowering=False)
    dr = {}

    def din(name, shape, dtype=F32):
        dr[name] = nc.dram_tensor(name, list(shape), dtype, kind="ExternalInput").ap()
        return dr[name]

    x_in = din("x_in", (TPC + 2 * P, D))
    convw = din("convw", (3, D, CONVO))
    convb = din("convb", (CONVO, 1))
    wq_in = din("wq_in", (W_, W_))
    wk_in = din("wk_in", (W_, W_))
    wv_in = din("wv_in", (W_, W_))
    wo_in = din("wo_in", (W_, W_))
    bq_in = din("bq_in", (W_, 1))
    bk_in = din("bk_in", (W_, 1))
    bv_in = din("bv_in", (1, W_))
    bo_in = din("bo_in", (W_, 1))
    projw = din("projw", (W_, D))
    projb = din("projb", (P, KC))
    gb_in = din("gb_in", (FC, P, KC * 64), U8)
    ub_in = din("ub_in", (FC, P, KC * 64), U8)
    db_in = din("db_in", (8, FC, P, 256), U8)
    gs_in = din("gs_in", (KC, DFF))
    us_in = din("us_in", (KC, DFF))
    ds_in = din("ds_in", (FC, D))
    gz_in = din("gz_in", (KC, DFF // 2), U8)
    uz_in = din("uz_in", (KC, DFF // 2), U8)
    dz_in = din("dz_in", (FC, D // 2), U8)

    out_d = nc.dram_tensor("out_d", [D, TPC], F32, kind="ExternalOutput").ap()
    y1_sp = nc.dram_tensor("y1_spill", [NT, D, MT], F32,
                           kind=("ExternalOutput" if stop == "y1" else "Internal")).ap()
    at_sp = nc.dram_tensor("attn_spill", [D, MT], F32,
                           kind=("ExternalOutput" if stop in ("attn_sp", "ffn_sp")
                                 else "Internal")).ap()
    zsg_d = nc.dram_tensor("zsg_d", [KC, DFF], BF16, kind="Internal").ap()
    zsu_d = nc.dram_tensor("zsu_d", [KC, DFF], BF16, kind="Internal").ap()
    zsd_d = nc.dram_tensor("zsd_d", [FC, D], BF16, kind="Internal").ap()

    dbg = {}
    st = ExitStack()
    with st:
        AR1 = st.enter_context(nc.sbuf_tensor("AR1", [P, 66560], U8))
        AR2 = st.enter_context(nc.sbuf_tensor("AR2", [P, 90112], U8))
        AR3 = st.enter_context(nc.sbuf_tensor("AR3", [P, 33536], U8))
        identf = st.enter_context(nc.sbuf_tensor("identf", [P, P], F32))
        identb = st.enter_context(nc.sbuf_tensor("identb", [P, P], BF16))
        onesr = st.enter_context(nc.sbuf_tensor("onesr", [1, P], F32))
        onesr_b = st.enter_context(nc.sbuf_tensor("onesr_b", [1, P], BF16))
        onescol_b = st.enter_context(nc.sbuf_tensor("onescol_b", [P, 1], BF16))
        c32 = st.enter_context(nc.sbuf_tensor("c32", [P, 64], BF16))
        c86 = st.enter_context(nc.sbuf_tensor("c86", [P, 172], BF16))
        convbuf = st.enter_context(nc.sbuf_tensor("convbuf", [P, 2 * 1024], BF16))
        biases = st.enter_context(nc.sbuf_tensor("biases", [P, 80], F32))
        lnsc = st.enter_context(nc.sbuf_tensor("lnsc", [P, 16], F32))
        attn_mx = st.enter_context(nc.sbuf_tensor("attn_mx", [P, 3 * H], F32))
        xg_sb = st.enter_context(nc.sbuf_tensor("xg_sb", [KC, MT], BF16))
        hg_sb = st.enter_context(nc.sbuf_tensor("hg_sb", [FC, MT], BF16))
        r512 = st.enter_context(nc.sbuf_tensor("r512", [1, 4 * MT], F32))
        constt = st.enter_context(nc.sbuf_tensor("constt", [P, 2], F32))
        PSA_t = st.enter_context(nc.psum_tensor("PSA", [P, 4096], F32))
        PSA = PSA_t[:]
        PSAb = PSA_t[:].bitcast(BF16)

        S = Sched(nc, st)
        for e in ENGINES:
            S.new_sem("e_" + e)
        qX = DmaQ(S, "sync", "dX")
        qY = DmaQ(S, "sync", "dY")
        qB = DmaQ(S, "sync", "dB")
        qW = DmaQ(S, "gpsimd", "dW")

        def carve(arena, off_bytes, pcount, fcount, dtype):
            esz = {F32: 4, BF16: 2, U8: 1}[dtype]
            ap = arena[:].bitcast(dtype)
            assert off_bytes % esz == 0, (off_bytes, dtype)
            o = off_bytes // esz
            assert (o + fcount) * esz <= arena.shape[1], \
                f"carve overflow: {off_bytes}+{fcount * esz} > {arena.shape[1]}"
            return ap[0:pcount, o:o + fcount]

        x_blk = [carve(AR1, b * 16384, P, D, F32) for b in range(NTB)]
        gu_byte = [carve(AR1, i * 2048, P, 2048, U8) for i in range(2)]
        gu_nib = [carve(AR1, 4096 + i * 4096, P, 4096, U8) for i in range(2)]
        gu_w = [carve(AR1, 12288 + i * 8192, P, 4096, BF16) for i in range(2)]
        sel32 = carve(AR1, 28672, KC, KC * P, BF16)
        ssl = [carve(AR1, 36864 + i * 256, KC, P, BF16) for i in range(2)]
        zsl = [carve(AR1, 37376 + i * 256, KC, P, BF16) for i in range(2)]
        silu_sb = carve(AR1, 38912, P, MT, F32)
        zb_scr = carve(AR1, 0, KC, DFF // 2, U8)
        zn_scr = carve(AR1, 5504, KC, DFF, U8)
        zs_scr = carve(AR1, 16512, KC, DFF // 4, BF16)
        zb86_scr = carve(AR1, 0, FC, D // 2, U8)
        zn86_scr = carve(AR1, 5504, FC, D, U8)
        zs86_scr = carve(AR1, 16512, FC, D // 2, BF16)
        dn_byte = [carve(AR1, i * 1024, P, 1024, U8) for i in range(2)]
        dn_nib = [carve(AR1, 2048 + i * 512, P, 512, U8) for i in range(2)]
        dn_w = [carve(AR1, 3072 + i * 1024, P, 512, BF16) for i in range(2)]
        dsl = [carve(AR1, 5120 + i * 1024, 1, MT, BF16) for i in range(2)]
        nzd = [carve(AR1, 7168 + i * 256, FC, P, BF16) for i in range(2)]
        st_sq = carve(AR1, 8192, P, MT, BF16)
        st_cp = carve(AR1, 9216, P, MT, BF16)
        ffn_stage = [carve(AR1, 10240 + i * 2048, P, MT, F32) for i in range(2)]

        ACTW = 516
        actT = carve(AR3, 0, P, KC * ACTW, BF16)

        h_fm = [carve(AR2, k * 1024, P, MT, BF16) for k in range(FC)]
        qkv = [carve(AR2, i * 1024, P, MT, BF16) for i in range(NCO)]
        qh_s = [carve(AR2, 24576 + i * 1024, P, MT, BF16) for i in range(HEC)]
        kh_s = [carve(AR2, 32768 + i * 1024, P, MT, BF16) for i in range(HEC)]
        vh_tm = [carve(AR2, 40960 + tb * 2048, P, W_, BF16) for tb in range(NTB)]
        o_fm = [carve(AR2, 49152 + i * 1024, P, MT, BF16) for i in range(HEC)]
        o_fm_all = carve(AR2, 49152, P, HEC * MT, BF16)
        wmat = carve(AR2, 57344, P, HEC * 1024, BF16)
        wmat2 = carve(AR2, 0, P, HEC * 1024, BF16)
        ao_buf = [carve(AR2, 69632 + i * 1024, P, MT, BF16) for i in range(HEC)]
        mask16 = carve(AR2, 73728, P, H * P, F32)
        bv_bc = carve(AR2, 81920, P, W_, F32)
        bvrow = carve(AR2, 86016, 1, W_, F32)
        xh_blk = carve(AR2, 0, P, D, F32)
        sqjunk = carve(AR2, 16384, P, D, BF16)
        xln_scr = carve(AR2, 24576, P, D, BF16)
        attn_sc_v = carve(AR2, 0, P, H * P, F32)
        attn_e_v = carve(AR2, 8192, P, H * P, F32)
        attn_p_v = carve(AR2, 16384, P, H * P, BF16)
        attn_pT_v = carve(AR2, 20480, P, H * P, BF16)
        qh_od = [carve(AR2, 81920 + i * 1024, 64, MT, BF16) for i in range(HEC)]
        kh_od = [carve(AR2, 57344 + i * 1024, 64, MT, BF16) for i in range(HEC)]
        af_buf = [carve(AR2, nc * 2048, P, MT, F32) for nc in range(KC)]
        pjw_b = [carve(AR2, 65536 + i * 2048, P, HEC * P, BF16) for i in range(2)]
        xfm_st = carve(AR2, 77824, P, MT, F32)
        y1_st = carve(AR2, 79872, P, MT, F32)
        y1_ld = [carve(AR2, 81920 + i * 2048, P, MT, F32) for i in range(2)]
        p7_cp = carve(AR2, 86016, P, MT, BF16)
        p7_sq = carve(AR2, 87040, P, MT, BF16)
        y2_st = [carve(AR2, 65536 + i * 2048, P, MT, F32) for i in range(2)]
        ff_buf = [carve(AR1, 24576 + nc * 2048, P, MT, F32) if nc < 16 else
                  carve(AR3, (nc - 16) * 2048, P, MT, F32) for nc in range(KC)]

        g = "gpsimd"

        export_chain = [None]

        def emit_export(name, ap, pcount, fcount, is_bf16, waits, tmp_off=40960):
            tpn = "dbg_" + name
            tp = nc.dram_tensor(tpn, [pcount, fcount], F32, kind="ExternalOutput").ap()
            if not is_bf16:
                tok = qX.dma(lambda e: e.dma_start(out=tp, in_=ap), waits=waits)
                dbg[name] = tpn
                export_chain[0] = tok
                return
            step = 2048
            tmp = carve(AR1, tmp_off, P, step, F32)
            last = export_chain[0]
            for o in range(0, fcount, step):
                wdt = min(step, fcount - o)
                ct = S.op("scalar", lambda e, o=o, wdt=wdt, ap=ap: e.activation(
                    out=tmp[0:pcount, 0:wdt], in_=ap[0:pcount, o:o + wdt],
                    func=AF.Copy), waits=list(waits) + ([last] if last else []),
                    inc=True)
                last = qX.dma(lambda e, o=o, wdt=wdt: e.dma_start(
                    out=tp[0:pcount, o:o + wdt], in_=tmp[0:pcount, 0:wdt]),
                    waits=[ct])
            dbg[name] = tpn
            export_chain[0] = last

        for ident, dt in ((identf, F32), (identb, BF16)):
            S.op(g, lambda e, ident=ident: e.memset(ident[:], 0.0))
            S.op(g, lambda e, ident=ident: e.affine_select(
                out=ident[:], in_=ident[:], compare_op=ALU.not_equal, fill=1.0,
                base=0, pattern=[[-1, P]], channel_multiplier=1))
        S.op(g, lambda e: e.memset(onesr[:], 1.0))
        S.op(g, lambda e: e.memset(onesr_b[:], 1.0))
        S.op(g, lambda e: e.memset(onescol_b[:], 1.0))
        S.op(g, lambda e: e.memset(constt[:, 0:1], EPS))
        S.op(g, lambda e: e.memset(constt[:, 1:2], 0.125))
        S.op(g, lambda e: e.memset(c32[:], 0.0))
        S.op(g, lambda e: e.memset(c32[:, 32:33], 1.0))
        S.op(g, lambda e: e.memset(c86[:], 0.0))
        kconst = S.op(g, lambda e: e.memset(c86[:, 86:87], 1.0), inc=True)

        last_b = None
        for ncx in range(NCO):
            last_b = qW.dma(lambda e, ncx=ncx: e.dma_start(
                out=biases[:, ncx:ncx + 1], in_=convb[ncx * P:(ncx + 1) * P, :]))
        for i, src in ((0, bq_in), (1, bk_in), (2, bo_in)):
            for hc in range(HEC):
                last_b = qW.dma(lambda e, i=i, hc=hc, src=src: e.dma_start(
                    out=biases[:, 24 + i * 8 + hc:24 + i * 8 + hc + 1],
                    in_=src[hc * P:(hc + 1) * P, :]))
        last_b = qW.dma(lambda e: e.dma_start(
            out=biases[:, 48:80], in_=projb))
        bias_tok = S.op("vector", lambda e: e.tensor_scalar(
            out=biases[:, 24:32], in0=biases[:, 24:32], scalar1=0.125, scalar2=None,
            op0=ALU.mult), waits=[last_b], inc=True)

        def ext4(eng, byte_ap, nib_ap, Wrd, waits, inc_last=True):
            bq = byte_ap.rearrange("p (w t f) -> p w t f", w=Wrd, t=2, f=2)
            nv = nib_ap.rearrange("p (w q t) -> p w q t", w=Wrd, q=4, t=2)
            ops = [(ALU.bitwise_and, 15, 0, 0),
                   (ALU.bitwise_and, 15, 1, 2),
                   (ALU.logical_shift_right, 4, 0, 1),
                   (ALU.logical_shift_right, 4, 1, 3)]
            tok = None
            for idx, (alu, scl, fsel, qb) in enumerate(ops):
                iv = bq[:, :, :, fsel:fsel + 1]
                ov = nv[:, :, qb:qb + 1, :]
                tok = S.op(eng, lambda e, iv=iv, ov=ov, alu=alu, scl=scl:
                           e.tensor_single_scalar(out=ov, in_=iv, scalar=scl, op=alu),
                           waits=waits, inc=(inc_last and idx == 3))
            return tok

        zsp_tok = None
        for (zin, sin, zout) in ((gz_in, gs_in, zsg_d), (uz_in, us_in, zsu_d)):
            zb_tok = qX.dma(lambda e, zin=zin: e.dma_start(out=zb_scr, in_=zin),
                            waits=[S.tok("vector")])
            etok = ext4("vector", zb_scr, zn_scr, DFF // 8, [zb_tok, kconst])
            for q4 in range(4):
                w0, w1 = q4 * (DFF // 4), (q4 + 1) * (DFF // 4)
                s_tok = qW.dma(lambda e, sin=sin, w0=w0, w1=w1: e.dma_start(
                    out=zs_scr, in_=sin[:, w0:w1]),
                    waits=[S.tok("vector")] + ([zsp_tok] if zsp_tok else []))
                m_tok = S.op("vector", lambda e, w0=w0, w1=w1: e.tensor_tensor(
                    out=zs_scr, in0=zn_scr[:, w0:w1], in1=zs_scr, op=ALU.mult),
                    waits=[etok, s_tok], inc=True)
                m_tok = S.op("vector", lambda e: e.tensor_scalar(
                    out=zs_scr, in0=zs_scr, scalar1=-1.0, scalar2=None, op0=ALU.mult),
                    waits=[m_tok], inc=True)
                zsp_tok = qY.dma(lambda e, zout=zout, w0=w0, w1=w1: e.dma_start(
                    out=zout[:, w0:w1], in_=zs_scr), waits=[m_tok])
        zb_tok = qX.dma(lambda e: e.dma_start(out=zb86_scr, in_=dz_in),
                        waits=[S.tok("vector")])
        etok = ext4("vector", zb86_scr, zn86_scr, D // 8, [zb_tok, kconst])
        for q2 in range(2):
            w0, w1 = q2 * (D // 2), (q2 + 1) * (D // 2)
            s_tok = qW.dma(lambda e, w0=w0, w1=w1: e.dma_start(
                out=zs86_scr, in_=ds_in[:, w0:w1]),
                waits=[S.tok("vector")] + ([zsp_tok] if zsp_tok else []))
            m_tok = S.op("vector", lambda e, w0=w0, w1=w1: e.tensor_tensor(
                out=zs86_scr, in0=zn86_scr[:, w0:w1], in1=zs86_scr, op=ALU.mult),
                waits=[etok, s_tok], inc=True)
            m_tok = S.op("vector", lambda e: e.tensor_scalar(
                out=zs86_scr, in0=zs86_scr, scalar1=-1.0, scalar2=None, op0=ALU.mult),
                waits=[m_tok], inc=True)
            zsp_tok = qY.dma(lambda e, w0=w0, w1=w1: e.dma_start(
                out=zsd_d[:, w0:w1], in_=zs86_scr), waits=[m_tok])
        zs_ready = qY.final_waits()

        def ln_stats_tm(T_ap, waits):
            v, s = "vector", "scalar"
            t1 = S.op(v, lambda e: e.reduce_sum(lnsc[:, 2:3], T_ap, axis=AX.X),
                      waits=waits, inc=True)
            t2 = S.op(s, lambda e: e.activation(
                out=sqjunk, in_=T_ap, func=AF.Square, accum_out=lnsc[:, 3:4]),
                waits=waits, inc=True)
            t1 = S.op(v, lambda e: e.tensor_scalar(
                out=lnsc[:, 0:1], in0=lnsc[:, 2:3], scalar1=INV_D, scalar2=None,
                op0=ALU.mult), waits=[t1, t2], inc=True)
            t1 = S.op(v, lambda e: e.tensor_tensor(
                out=lnsc[:, 4:5], in0=lnsc[:, 0:1], in1=lnsc[:, 0:1], op=ALU.mult),
                waits=[t1], inc=True)
            t1 = S.op(v, lambda e: e.scalar_tensor_tensor(
                out=lnsc[:, 5:6], in0=lnsc[:, 3:4], scalar=INV_D, in1=lnsc[:, 4:5],
                op0=ALU.mult, op1=ALU.subtract), waits=[t1], inc=True)
            t2 = S.op(s, lambda e: e.activation(
                out=lnsc[:, 6:7], in_=lnsc[:, 5:6], func=AF.Sqrt, bias=constt[:, 0:1]),
                waits=[t1], inc=True)
            t1 = S.op(v, lambda e: e.reciprocal(lnsc[:, 1:2], lnsc[:, 6:7]),
                      waits=[t2], inc=True)
            return t1

        def transpose_block(src_bf_ap, dst_col_fn, wait_src):
            evac_toks = []
            for kcq in range(KC // 4):
                bank = kcq % 2
                pso = PSAb[:, bank * 1024: bank * 1024 + 1024]
                for j in range(4):
                    kc = kcq * 4 + j
                    w = [wait_src]
                    if kcq >= 2:
                        w.append(evac_toks[kcq - 2])
                    S.op("tensor", lambda e, kc=kc, j=j, pso=pso: e.transpose(
                        pso[:, j * P:(j + 1) * P],
                        src_bf_ap[:, kc * P:(kc + 1) * P], identb[:]),
                        waits=w, inc=(j == 3))
                tt = S.tok("tensor")
                for j in range(4):
                    kc = kcq * 4 + j
                    dap, width, soff = dst_col_fn(kc)
                    S.op("scalar", lambda e, dap=dap, width=width, soff=soff, j=j,
                         pso=pso: e.activation(
                             out=dap, in_=pso[:, j * P + soff: j * P + soff + width],
                             func=AF.Copy), waits=[tt], inc=(j == 3))
                evac_toks.append(S.tok("scalar"))
            return evac_toks[-1]

        def ln_row_scalars(sum_ap, sq_ap, rb_slice, mb_slice, waits):
            v, s = "vector", "scalar"
            t1 = S.op(v, lambda e: e.tensor_scalar(
                out=r512[:, 0:512], in0=sum_ap, scalar1=INV_D, scalar2=None,
                op0=ALU.mult), waits=waits, inc=True)
            t1 = S.op(v, lambda e: e.tensor_tensor(
                out=r512[:, 1536:2048], in0=r512[:, 0:512], in1=r512[:, 0:512],
                op=ALU.mult), waits=[t1], inc=True)
            t1 = S.op(v, lambda e: e.scalar_tensor_tensor(
                out=r512[:, 1536:2048], in0=sq_ap, scalar=INV_D,
                in1=r512[:, 1536:2048], op0=ALU.mult, op1=ALU.subtract),
                waits=[t1], inc=True)
            t2 = S.op(s, lambda e: e.activation(
                out=r512[:, 1536:2048], in_=r512[:, 1536:2048], func=AF.Sqrt,
                bias=constt[0:1, 0:1]), waits=[t1], inc=True)
            t1 = S.op(v, lambda e: e.reciprocal(
                r512[:, 512:1024], r512[:, 1536:2048]), waits=[t2], inc=True)
            t1 = S.op(v, lambda e: e.tensor_tensor(
                out=r512[:, 1024:1536], in0=r512[:, 0:512], in1=r512[:, 512:1024],
                op=ALU.mult), waits=[t1], inc=True)
            t1 = S.op(v, lambda e: e.tensor_scalar(
                out=r512[:, 1024:1536], in0=r512[:, 1024:1536], scalar1=-1.0,
                scalar2=None, op0=ALU.mult), waits=[t1], inc=True)
            S.op("tensor", lambda e, rb=rb_slice: e.matmul(
                PSA[:, rb], onesr[:], r512[:, 512:1024], start=True, stop=True),
                waits=[t1])
            S.op("tensor", lambda e, mb=mb_slice: e.matmul(
                PSA[:, mb], onesr[:], r512[:, 1024:1536], start=True, stop=True),
                waits=[t1], inc=True)
            return S.tok("tensor")

        stopped = False
        for t in range(NT):
            if stopped:
                break
            tok0 = t * MT
            xrow0 = P + tok0
            sfx = "" if t == 0 else "@1"

            x_toks = []
            for b in range(NTB):
                x_toks.append(qX.dma(lambda e, b=b, xrow0=xrow0: e.dma_start(
                    out=x_blk[b], in_=x_in[xrow0 + b * P: xrow0 + (b + 1) * P, :]),
                    waits=zs_ready + [S.tok("vector"), S.tok("scalar"),
                                      S.tok("tensor")]))
            for b in range(NTB):
                stok = ln_stats_tm(x_blk[b], [x_toks[b], bias_tok, kconst])
                atok = S.op("vector", lambda e, b=b: e.tensor_scalar(
                    out=xln_scr, in0=x_blk[b], scalar1=lnsc[:, 0:1],
                    scalar2=lnsc[:, 1:2], op0=ALU.subtract, op1=ALU.mult),
                    waits=[stok], inc=True)
                tr_tok = transpose_block(
                    xln_scr,
                    lambda kc, b=b: (
                        actT[:, kc * ACTW + 1 + b * P: kc * ACTW + 1 + (b + 1) * P],
                        P, 0), atok)
            for side in range(2):
                if side == 0:
                    r0, r1 = xrow0 - P, xrow0
                    dstf = lambda kc: (actT[:, kc * ACTW: kc * ACTW + 1], 1, P - 1)
                else:
                    r0, r1 = xrow0 + MT, xrow0 + MT + P
                    dstf = lambda kc: (actT[:, kc * ACTW + 513: kc * ACTW + 514], 1, 0)
                htok = qX.dma(lambda e, r0=r0, r1=r1: e.dma_start(
                    out=xh_blk, in_=x_in[r0:r1, :]),
                    waits=[S.tok("scalar"), S.tok("vector")])
                stok = ln_stats_tm(xh_blk, [htok])
                atok = S.op("vector", lambda e: e.tensor_scalar(
                    out=xln_scr, in0=xh_blk, scalar1=lnsc[:, 0:1],
                    scalar2=lnsc[:, 1:2], op0=ALU.subtract, op1=ALU.mult),
                    waits=[stok], inc=True)
                tr_tok = transpose_block(xln_scr, dstf, atok)
            tr_all = tr_tok
            if stop == "actT" + sfx:
                emit_export("actT", actT, P, KC * ACTW, True, [tr_all])
                stopped = True
                break

            order = [(ncg, kc, w) for ncg in range(3) for kc in range(KC)
                     for w in range(3)]
            wtoks = {}
            mm_toks = {}

            def conv_dma(gi):
                ncg, kc, w = order[gi]
                buf = convbuf[:, (gi % 2) * 1024:(gi % 2) * 1024 + 1024]
                wwaits = [tr_all]
                if gi >= 2:
                    wwaits.append(mm_toks[gi - 2])
                wtoks[gi] = qW.dma(lambda e, kc=kc, w=w, buf=buf, ncg=ncg:
                                   e.dma_start(
                    out=buf, in_=convw[w, kc * P:(kc + 1) * P,
                                       ncg * 1024:(ncg + 1) * 1024]), waits=wwaits)

            conv_dma(0)
            conv_dma(1)
            qkv_tok = None
            for gi, (ncg, kc, w) in enumerate(order):
                buf = convbuf[:, (gi % 2) * 1024:(gi % 2) * 1024 + 1024]
                i_in_ncg = gi - ncg * 96
                ev_wait = S.tok("scalar")
                for nc8 in range(8):
                    S.op("tensor", lambda e, buf=buf, nc8=nc8, kc=kc, w=w,
                         i=i_in_ncg: e.matmul(
                             PSA[:, nc8 * 512:(nc8 + 1) * 512],
                             buf[:, nc8 * P:(nc8 + 1) * P],
                             actT[:, kc * ACTW + w: kc * ACTW + w + MT],
                             start=(i == 0), stop=(i == 95)),
                         waits=[wtoks[gi], ev_wait], inc=(nc8 == 7))
                mm_toks[gi] = S.tok("tensor")
                if gi + 2 < len(order):
                    conv_dma(gi + 2)
                if i_in_ncg == 95:
                    for nc8 in range(8):
                        ncx = ncg * 8 + nc8
                        qkv_tok = S.op("scalar", lambda e, nc8=nc8, ncx=ncx:
                                       e.activation(
                            out=qkv[ncx], in_=PSA[:, nc8 * 512:(nc8 + 1) * 512],
                            func=AF.Identity, bias=biases[:, ncx:ncx + 1]),
                            waits=[mm_toks[gi]], inc=True)
            qkv_done = qkv_tok
            if stop == "qkv" + sfx:
                emit_export("qkv", carve(AR2, 0, P, NCO * MT, BF16), P, NCO * MT,
                            True, [qkv_done])
                stopped = True
                break

            def load_wmat(src, waits, dst=None):
                dst = wmat if dst is None else dst
                tok = None
                for c in range(HEC):
                    tok = qW.dma(lambda e, src=src, c=c, dst=dst: e.dma_start(
                        out=dst[:, c * 1024:(c + 1) * 1024],
                        in_=src[c * P:(c + 1) * P, :]),
                        waits=(waits if c == 0 else ()))
                return tok

            def proj_heads(wtok, qkv_off, outbuf, bias_base, scale):
                done = []
                for hc in range(HEC):
                    bank = hc % 2
                    for kc in range(HEC):
                        S.op("tensor", lambda e, hc=hc, kc=kc, bank=bank: e.matmul(
                            PSA[:, bank * 512:(bank + 1) * 512],
                            wmat[:, kc * 1024 + hc * P: kc * 1024 + (hc + 1) * P],
                            qkv[qkv_off + kc], start=(kc == 0), stop=(kc == HEC - 1)),
                            waits=[wtok] + ([done[hc - 2]] if hc >= 2 else []),
                            inc=(kc == HEC - 1))
                    mt_ = S.tok("tensor")
                    done.append(S.op("scalar", lambda e, hc=hc, bank=bank:
                                     e.activation(
                        out=outbuf[hc], in_=PSA[:, bank * 512:(bank + 1) * 512],
                        func=AF.Identity, bias=biases[:, bias_base + hc:bias_base + hc + 1],
                        scale=scale), waits=[mt_], inc=True))
                return done[-1]

            wq_tok = load_wmat(wq_in, [qkv_done])
            qh_tok = proj_heads(wq_tok, 0, qh_s, 24, constt[:, 1:2])
            wk_tok = load_wmat(wk_in, [qh_tok])
            kh_tok = proj_heads(wk_tok, 8, kh_s, 32, 1.0)
            wv_tok = load_wmat(wv_in, [kh_tok])
            bv_tok = qW.dma(lambda e: e.dma_start(out=bvrow, in_=bv_in),
                            waits=[kh_tok])
            for i2 in range(2):
                S.op("tensor", lambda e, i2=i2: e.matmul(
                    PSA[:, 1024 + i2 * 512: 1024 + (i2 + 1) * 512],
                    onesr[:], bvrow[:, i2 * 512:(i2 + 1) * 512],
                    start=True, stop=True), waits=[bv_tok, kh_tok], inc=(i2 == 1))
            bvb_t = S.tok("tensor")
            bv_ev = S.op("vector", lambda e: e.tensor_copy(
                out=bv_bc, in_=PSA[:, 1024:2048]), waits=[bvb_t], inc=True)
            vh_done = []
            for tb in range(NTB):
                for half in range(2):
                    bank = half
                    for kc in range(HEC):
                        S.op("tensor", lambda e, tb=tb, half=half, kc=kc, bank=bank:
                             e.matmul(
                                 PSA[:, bank * 512:(bank + 1) * 512],
                                 qkv[16 + kc][:, tb * P:(tb + 1) * P],
                                 wmat[:, kc * 1024 + half * 512:
                                      kc * 1024 + (half + 1) * 512],
                                 start=(kc == 0), stop=(kc == HEC - 1)),
                             waits=[wv_tok, bv_ev] +
                                   ([vh_done[-2]] if len(vh_done) >= 2 else []),
                             inc=(kc == HEC - 1))
                    mt_ = S.tok("tensor")
                    vh_done.append(S.op("vector",
                        lambda e, tb=tb, half=half, bank=bank: e.tensor_tensor(
                            out=vh_tm[tb][:, half * 512:(half + 1) * 512],
                            in0=PSA[:, bank * 512:(bank + 1) * 512],
                            in1=bv_bc[:, half * 512:(half + 1) * 512], op=ALU.add),
                        waits=[mt_], inc=True))
            vh_all = vh_done[-1]
            if stop == "vh" + sfx:
                emit_export("qh", carve(AR2, 24576, P, HEC * MT, BF16), P,
                            HEC * MT, True, [vh_all])
                emit_export("kh", carve(AR2, 32768, P, HEC * MT, BF16), P,
                            HEC * MT, True, [vh_all])
                emit_export("vh", carve(AR2, 40960, P, NTB * W_, BF16), P,
                            NTB * W_, True, [vh_all])
                stopped = True
                break

            od_tok = None
            for hc in range(HEC):
                od_tok = qX.dma(lambda e, hc=hc: e.dma_start(
                    out=qh_od[hc][:], in_=qh_s[hc][64:128, :]),
                    waits=[qh_tok, vh_all])
                od_tok = qX.dma(lambda e, hc=hc: e.dma_start(
                    out=kh_od[hc][:], in_=kh_s[hc][64:128, :]),
                    waits=[kh_tok, vh_all])
            S.op(g, lambda e: e.memset(mask16, 0.0), waits=[vh_all, S.tok("tensor")])
            mk = S.op(g, lambda e: e.affine_select(
                out=mask16, in_=mask16, compare_op=ALU.is_ge, fill=NEG,
                base=0, pattern=[[0, H], [-1, P]], channel_multiplier=1), inc=True)
            attn_last = None
            mask_done = None
            pt_done = None
            exp_done = None
            trp_done = None
            for tb in range(NTB):
                for hh in range(H):
                    hc, hp = hh // 2, (hh % 2) * 64
                    qsl = (qh_s[hc][0:64, tb * P:(tb + 1) * P] if hp == 0
                           else qh_od[hc][:, tb * P:(tb + 1) * P])
                    ksl = (kh_s[hc][0:64, tb * P:(tb + 1) * P] if hp == 0
                           else kh_od[hc][:, tb * P:(tb + 1) * P])
                    S.op("tensor", lambda e, hh=hh, qsl=qsl, ksl=ksl: e.matmul(
                        PSA[:, hh * P:(hh + 1) * P], qsl, ksl,
                        start=True, stop=True, skip_group_check=True),
                        waits=[mk, vh_all, od_tok] +
                              ([mask_done] if mask_done else []),
                        inc=(hh == H - 1))
                sc_t = S.tok("tensor")
                v_t = S.op("vector", lambda e: e.tensor_tensor(
                    out=attn_sc_v, in0=PSA[:, 0:2048], in1=mask16, op=ALU.add),
                    waits=[sc_t] + ([exp_done] if exp_done else []), inc=True)
                mask_done = v_t
                v_t = S.op("vector", lambda e: e.tensor_reduce(
                    out=attn_mx[:, 0:H],
                    in_=attn_sc_v.rearrange("p (h k) -> p h k", h=H),
                    axis=AX.X, op=ALU.max, negate=True), waits=[v_t], inc=True)
                for hh in range(H):
                    S.op("scalar", lambda e, hh=hh: e.activation(
                        out=attn_e_v[:, hh * P:(hh + 1) * P],
                        in_=attn_sc_v[:, hh * P:(hh + 1) * P],
                        func=AF.Exp, bias=attn_mx[:, hh:hh + 1]),
                        waits=[v_t], inc=(hh == H - 1))
                s_t = S.tok("scalar")
                exp_done = s_t
                v_t = S.op("vector", lambda e: e.tensor_reduce(
                    out=attn_mx[:, H:2 * H],
                    in_=attn_e_v.rearrange("p (h k) -> p h k", h=H),
                    axis=AX.X, op=ALU.add), waits=[s_t], inc=True)
                v_t = S.op("vector", lambda e: e.reciprocal(
                    attn_mx[:, 2 * H:3 * H], attn_mx[:, H:2 * H]),
                    waits=[v_t], inc=True)
                for hh in range(H):
                    S.op("vector", lambda e, hh=hh: e.tensor_scalar(
                        out=attn_p_v[:, hh * P:(hh + 1) * P],
                        in0=attn_e_v[:, hh * P:(hh + 1) * P],
                        scalar1=attn_mx[:, 2 * H + hh:2 * H + hh + 1], scalar2=None,
                        op0=ALU.mult),
                        waits=[v_t] + ([trp_done] if trp_done else []),
                        inc=(hh == H - 1))
                v_t = S.tok("vector")
                for hh in range(H):
                    S.op("tensor", lambda e, hh=hh: e.transpose(
                        PSAb[:, 4096 + hh * P: 4096 + (hh + 1) * P],
                        attn_p_v[:, hh * P:(hh + 1) * P], identb[:]),
                        waits=[v_t] + ([pt_done] if pt_done else []),
                        inc=(hh == H - 1))
                t_t = S.tok("tensor")
                trp_done = t_t
                s_t = S.op("scalar", lambda e: e.activation(
                    out=attn_pT_v, in_=PSAb[:, 4096:6144], func=AF.Copy),
                    waits=[t_t], inc=True)
                pt_done = s_t
                for hh in range(H):
                    hc, hp = hh // 2, (hh % 2) * 64
                    S.op("tensor", lambda e, hh=hh, hc=hc, hp=hp, tb=tb: e.matmul(
                        PSA[hp:hp + 64, 3072 + hc * P: 3072 + (hc + 1) * P],
                        vh_tm[tb][:, hh * 64:(hh + 1) * 64],
                        attn_pT_v[:, hh * P:(hh + 1) * P],
                        start=True, stop=True, skip_group_check=True),
                        waits=[s_t], inc=(hh == H - 1))
                t_t = S.tok("tensor")
                attn_last = S.op("scalar", lambda e, tb=tb: e.activation(
                    out=o_fm_all.rearrange("p (c n) -> p c n", c=HEC)[
                        :, :, tb * P:(tb + 1) * P],
                    in_=PSA[:, 3072:4096].rearrange("p (c n) -> p c n", c=HEC),
                    func=AF.Copy), waits=[t_t], inc=True)
                if stop == "attn1" + sfx and tb == 0:
                    break
            if stop == "attn1" + sfx:
                emit_export("qod", carve(AR2, 81920, 64, HEC * MT, BF16), 64,
                            HEC * MT, True, [attn_last])
                emit_export("sc", attn_sc_v, P, H * P, False, [attn_last])
                emit_export("mx", attn_mx[:], P, 3 * H, False, [attn_last])
                emit_export("pv", attn_p_v, P, H * P, True, [attn_last])
                emit_export("pT", attn_pT_v, P, H * P, True, [attn_last])
                emit_export("o1", o_fm_all, P, HEC * MT, True, [attn_last])
                stopped = True
                break
            if stop == "o_fm" + sfx:
                emit_export("o_fm", o_fm_all, P, HEC * MT, True, [attn_last])
                stopped = True
                break

            wo_tok = load_wmat(wo_in, [attn_last], dst=wmat2)
            ao = ao_buf
            ao_done = []
            for wc in range(HEC):
                bank = wc % 2
                for hc in range(HEC):
                    S.op("tensor", lambda e, wc=wc, hc=hc, bank=bank: e.matmul(
                        PSA[:, bank * 512:(bank + 1) * 512],
                        wmat2[:, hc * 1024 + wc * P: hc * 1024 + (wc + 1) * P],
                        o_fm[hc], start=(hc == 0), stop=(hc == HEC - 1)),
                        waits=[wo_tok] + ([ao_done[wc - 2]] if wc >= 2 else []),
                        inc=(hc == HEC - 1))
                mt_ = S.tok("tensor")
                ao_done.append(S.op("scalar", lambda e, wc=wc, bank=bank:
                                    e.activation(
                    out=ao[wc], in_=PSA[:, bank * 512:(bank + 1) * 512],
                    func=AF.Identity, bias=biases[:, 40 + wc:40 + wc + 1]),
                    waits=[mt_], inc=True))
            ao_all = ao_done[-1]
            if stop == "ao" + sfx:
                emit_export("ao", carve(AR2, 69632, P, HEC * MT, BF16), P,
                            HEC * MT, True, [ao_all])
                stopped = True
                break

            pj_toks = {}
            at_done = []

            def pj_dma(nc_i, waits):
                buf = pjw_b[nc_i % 2]
                tok = None
                for c in range(HEC):
                    tok = qW.dma(lambda e, nc_i=nc_i, buf=buf, c=c: e.dma_start(
                        out=buf[:, c * P:(c + 1) * P],
                        in_=projw[c * P:(c + 1) * P, nc_i * P:(nc_i + 1) * P]),
                        waits=(waits if c == 0 else ()))
                pj_toks[nc_i] = tok

            pj_dma(0, [ao_all])
            pj_dma(1, [ao_all])
            for nc_i in range(KC):
                buf = pjw_b[nc_i % 2]
                bank = nc_i % 2
                for kc in range(HEC):
                    S.op("tensor", lambda e, nc_i=nc_i, kc=kc, buf=buf,
                         bank=bank: e.matmul(
                             PSA[:, bank * 512:(bank + 1) * 512],
                             buf[:, kc * P:(kc + 1) * P],
                             ao[kc],
                             start=(kc == 0), stop=(kc == HEC - 1)),
                         waits=[pj_toks[nc_i]] +
                               ([at_done[nc_i - 2]] if nc_i >= 2 else []),
                         inc=(kc == HEC - 1))
                mt_ = S.tok("tensor")
                if nc_i + 2 < KC:
                    pj_dma(nc_i + 2, [mt_])
                stg = af_buf[nc_i]
                ev = S.op("scalar", lambda e, nc_i=nc_i, bank=bank, stg=stg:
                          e.activation(
                              out=stg, in_=PSA[:, bank * 512:(bank + 1) * 512],
                              func=AF.Identity, bias=biases[:, 48 + nc_i:48 + nc_i + 1]),
                          waits=[mt_], inc=True)
                cp = S.op("scalar", lambda e, stg=stg: e.activation(
                    out=p7_cp, in_=stg, func=AF.Copy), waits=[ev], inc=True)
                sq = S.op("scalar", lambda e, stg=stg: e.activation(
                    out=p7_sq, in_=stg, func=AF.Square), waits=[cp], inc=True)
                S.op("tensor", lambda e, nc_i=nc_i: e.matmul(
                    PSA[0:1, 1024:1536], onescol_b[:], p7_cp[:],
                    start=(nc_i == 0), stop=(nc_i == KC - 1), skip_group_check=True),
                    waits=[cp])
                S.op("tensor", lambda e, nc_i=nc_i: e.matmul(
                    PSA[0:1, 1536:2048], onescol_b[:], p7_sq[:],
                    start=(nc_i == 0), stop=(nc_i == KC - 1), skip_group_check=True),
                    waits=[sq], inc=True)
                at_done.append(sq)
            if stop == "attn_sp" + sfx:
                for nc_i in range(KC):
                    qY.dma(lambda e, nc_i=nc_i: e.dma_start(
                        out=at_sp[nc_i * P:(nc_i + 1) * P, :], in_=af_buf[nc_i]),
                        waits=[at_done[-1], S.tok("tensor")])
                S.emit("sync", lambda e: e.nop(), waits=qY.final_waits())
                stopped = True
                break

            bc2 = ln_row_scalars(PSA[0:1, 1024:1536], PSA[0:1, 1536:2048],
                                 BK(6), BK(7),
                                 [S.tok("tensor"), at_done[-1]])
            y1_done = []
            for nc_i in range(KC):
                for tb in range(NTB):
                    S.op("tensor", lambda e, tb=tb, nc_i=nc_i: e.transpose(
                        PSA[:, 2048 + tb * P: 2048 + (tb + 1) * P],
                        x_blk[tb][:, nc_i * P:(nc_i + 1) * P], identf[:]),
                        waits=[bc2] + ([y1_done[nc_i - 2][0]] if nc_i >= 2 else []),
                        inc=(tb == NTB - 1))
                tx = S.tok("tensor")
                evx = S.op("vector", lambda e: e.tensor_copy(
                    out=xfm_st, in_=PSA[:, 2048:2560]), waits=[tx], inc=True)
                v1 = S.op("vector", lambda e, nc_i=nc_i: e.tensor_tensor(
                    out=y1_st, in0=af_buf[nc_i], in1=PSA[:, 3072:3584],
                    op=ALU.mult), waits=[bc2], inc=True)
                v1 = S.op("vector", lambda e: e.tensor_tensor(
                    out=y1_st, in0=y1_st, in1=PSA[:, 3584:4096], op=ALU.add),
                    waits=[v1], inc=True)
                v1 = S.op("vector", lambda e, nc_i=nc_i: e.tensor_tensor(
                    out=af_buf[nc_i], in0=y1_st, in1=xfm_st, op=ALU.add),
                    waits=[v1, evx], inc=True)
                cp = S.op("scalar", lambda e, nc_i=nc_i: e.activation(
                    out=p7_cp, in_=af_buf[nc_i], func=AF.Copy), waits=[v1], inc=True)
                sq = S.op("scalar", lambda e, nc_i=nc_i: e.activation(
                    out=p7_sq, in_=af_buf[nc_i], func=AF.Square),
                    waits=[cp], inc=True)
                S.op("tensor", lambda e, nc_i=nc_i: e.matmul(
                    PSA[0:1, 1024:1536], onescol_b[:], p7_cp[:],
                    start=(nc_i == 0), stop=(nc_i == KC - 1), skip_group_check=True),
                    waits=[cp])
                S.op("tensor", lambda e, nc_i=nc_i: e.matmul(
                    PSA[0:1, 1536:2048], onescol_b[:], p7_sq[:],
                    start=(nc_i == 0), stop=(nc_i == KC - 1), skip_group_check=True),
                    waits=[sq], inc=True)
                sp = qY.dma(lambda e, nc_i=nc_i, t=t: e.dma_start(
                    out=y1_sp[t, nc_i * P:(nc_i + 1) * P, :], in_=af_buf[nc_i]),
                    waits=[v1, S.tok("tensor")])
                y1_done.append((v1, sp))
            if stop == "y1" + sfx:
                S.emit("sync", lambda e: e.nop(), waits=[y1_done[-1][1]])
                stopped = True
                break

            bc3 = ln_row_scalars(PSA[0:1, 1024:1536], PSA[0:1, 1536:2048],
                                 BK(6), BK(7),
                                 [S.tok("tensor"), y1_done[-1][0]])
            x2_done = []
            for nc_i in range(KC):
                v1 = S.op("vector", lambda e, nc_i=nc_i: e.tensor_tensor(
                    out=y1_st, in0=af_buf[nc_i], in1=PSA[:, 3072:3584], op=ALU.mult),
                    waits=[bc3], inc=True)
                x2_done.append(S.op("vector", lambda e, nc_i=nc_i: e.tensor_tensor(
                    out=actT[:, nc_i * ACTW: nc_i * ACTW + MT], in0=y1_st,
                    in1=PSA[:, 3584:4096], op=ALU.add), waits=[v1], inc=True))
            x2_all = x2_done[-1]
            if stop == "xln2T" + sfx:
                emit_export("xln2T", actT, P, KC * ACTW, True, [x2_all])
                stopped = True
                break

            for kc in range(KC):
                S.op("tensor", lambda e, kc=kc: e.matmul(
                    PSA[0:KC, 0:512], c32[:, 32 - kc:64 - kc],
                    actT[:, kc * ACTW: kc * ACTW + MT],
                    start=(kc == 0), stop=(kc == KC - 1)),
                    waits=[x2_all], inc=(kc == KC - 1))
            xg_t = S.tok("tensor")
            xg_tok = S.op("scalar", lambda e: e.activation(
                out=xg_sb[:], in_=PSA[0:KC, 0:512], func=AF.Copy),
                waits=[xg_t], inc=True)

            S.op(g, lambda e: e.memset(sel32, 1.0), waits=[x2_all])
            S.op(g, lambda e: e.affine_select(
                out=sel32, in_=sel32, compare_op=ALU.is_ge, fill=0.0,
                base=0, pattern=[[1, KC * P]], channel_multiplier=-P))
            sel_t = S.op(g, lambda e: e.affine_select(
                out=sel32, in_=sel32, compare_op=ALU.is_ge, fill=0.0,
                base=127, pattern=[[-1, KC * P]], channel_multiplier=P), inc=True)

            y1_spilled = qY.final_waits()
            win_free = {4: None, 5: None}
            prev_ext = {0: None, 1: None}
            h_done = []
            for ns in range(FC):
                pair_tok = {}
                for mi, (bsrc, ssrc, zsrc) in enumerate(
                        ((ub_in, us_in, zsu_d), (gb_in, gs_in, zsg_d))):
                    bi = mi
                    bw = [sel_t, xg_tok, S.tok("vector")] + y1_spilled
                    if prev_ext[bi] is not None:
                        bw.append(prev_ext[bi])
                    bt = qB.dma(lambda e, bsrc=bsrc, ns=ns, bi=bi: e.dma_start(
                        out=gu_byte[bi], in_=bsrc[ns]), waits=bw)
                    st_ = qW.dma(lambda e, ssrc=ssrc, ns=ns, mi=mi: e.dma_start(
                        out=ssl[mi], in_=ssrc[:, ns * P:(ns + 1) * P]),
                        waits=[S.tok("tensor")])
                    zt_ = qB.dma(lambda e, zsrc=zsrc, ns=ns, mi=mi: e.dma_start(
                        out=zsl[mi], in_=zsrc[:, ns * P:(ns + 1) * P]),
                        waits=[S.tok("tensor")])
                    et = ext4("vector", gu_byte[bi], gu_nib[bi], KC * 16,
                              [bt, S.tok("vector")])
                    prev_ext[bi] = et
                    win = 4 + mi
                    for q8 in range(8):
                        for j4 in range(4):
                            kc = q8 * 4 + j4
                            bcw = [st_, sel_t]
                            if win_free[win] is not None:
                                bcw.append(win_free[win])
                            S.op("tensor", lambda e, kc=kc, j4=j4, win=win, mi=mi:
                                 e.matmul(
                                     PSA[:, win * 512 + j4 * P:
                                         win * 512 + (j4 + 1) * P],
                                     sel32[:, kc * P:(kc + 1) * P], ssl[mi][:],
                                     start=True, stop=True, skip_group_check=True),
                                 waits=bcw, inc=(j4 == 3))
                        bc_t = S.tok("tensor")
                        mtok = S.op("vector", lambda e, q8=q8, bi=bi, win=win:
                                    e.tensor_tensor(
                                out=gu_w[bi][:, q8 * 512:(q8 + 1) * 512],
                                in0=gu_nib[bi][:, q8 * 512:(q8 + 1) * 512],
                                in1=PSA[:, win * 512:(win + 1) * 512], op=ALU.mult),
                                waits=[et, bc_t], inc=True)
                        win_free[win] = mtok
                    wq_done = S.tok("vector")
                    accb = mi
                    S.op("tensor", lambda e, mi=mi, accb=accb: e.matmul(
                        PSA[:, accb * 512:(accb + 1) * 512],
                        zsl[mi][:], xg_sb[:],
                        start=True, stop=False, skip_group_check=True),
                        waits=[zt_, xg_tok, wq_done, S.tok("scalar")])
                    for kc in range(KC):
                        S.op("tensor", lambda e, kc=kc, bi=bi, accb=accb: e.matmul(
                            PSA[:, accb * 512:(accb + 1) * 512],
                            gu_w[bi][:, kc * P:(kc + 1) * P],
                            actT[:, kc * ACTW: kc * ACTW + MT],
                            start=False, stop=(kc == KC - 1),
                            skip_group_check=True),
                            waits=[wq_done], inc=(kc == KC - 1))
                    pair_tok[mi] = S.tok("tensor")
                sl = S.op("scalar", lambda e: e.activation(
                    out=silu_sb, in_=PSA[:, 512:1024], func=AF.Silu),
                    waits=[pair_tok[1]], inc=True)
                h_done.append(S.op("vector", lambda e, ns=ns: e.tensor_tensor(
                    out=h_fm[ns], in0=silu_sb, in1=PSA[:, 0:512], op=ALU.mult),
                    waits=[sl, pair_tok[0]], inc=True))
            h_all = h_done[-1]
            if stop == "h" + sfx:
                emit_export("h", carve(AR2, 0, P, FC * MT, BF16), P, FC * MT,
                            True, [h_all])
                stopped = True
                break

            for kc in range(FC):
                S.op("tensor", lambda e, kc=kc: e.matmul(
                    PSA[0:FC, 0:512], c86[:, 86 - kc:172 - kc], h_fm[kc],
                    start=(kc == 0), stop=(kc == FC - 1)),
                    waits=[h_all], inc=(kc == FC - 1))
            hg_t = S.tok("tensor")
            hg_tok = S.op("scalar", lambda e: e.activation(
                out=hg_sb[:], in_=PSA[0:FC, 0:512], func=AF.Copy),
                waits=[hg_t], inc=True)

            win_free = {4: None, 5: None}
            prev_extd = {0: None, 1: None}
            ffn_done = []
            for ncg in range(8):
                first_sc = S.tok("scalar")
                for kc in range(FC):
                    bi = kc % 2
                    bw = [hg_tok, S.tok("vector")]
                    if prev_extd[bi] is not None:
                        bw.append(prev_extd[bi])
                    bt = qB.dma(lambda e, ncg=ncg, kc=kc, bi=bi: e.dma_start(
                        out=dn_byte[bi][:, 0:256], in_=db_in[ncg, kc]), waits=bw)
                    st_ = qW.dma(lambda e, ncg=ncg, kc=kc, bi=bi: e.dma_start(
                        out=dsl[bi], in_=ds_in[kc:kc + 1,
                                              ncg * 512:(ncg + 1) * 512]),
                        waits=[S.tok("tensor")])
                    et = ext4("vector", dn_byte[bi][:, 0:256], dn_nib[bi], 64,
                              [bt, S.tok("vector")])
                    prev_extd[bi] = et
                    win = 4 + bi
                    bcw = [st_]
                    if win_free[win] is not None:
                        bcw.append(win_free[win])
                    S.op("tensor", lambda e, bi=bi, win=win: e.matmul(
                        PSA[:, win * 512:(win + 1) * 512], onesr_b[:], dsl[bi][:],
                        start=True, stop=True, skip_group_check=True),
                        waits=bcw, inc=True)
                    bc_t = S.tok("tensor")
                    wd_done = S.op("vector", lambda e, bi=bi, win=win:
                                   e.tensor_tensor(
                        out=dn_w[bi][:], in0=dn_nib[bi][:],
                        in1=PSA[:, win * 512:(win + 1) * 512], op=ALU.mult),
                        waits=[et, bc_t], inc=True)
                    win_free[win] = wd_done
                    for nc4 in range(4):
                        S.op("tensor", lambda e, kc=kc, nc4=nc4, bi=bi: e.matmul(
                            PSA[:, nc4 * 512:(nc4 + 1) * 512],
                            dn_w[bi][:, nc4 * P:(nc4 + 1) * P], h_fm[kc],
                            start=(kc == 0), stop=False, skip_group_check=True),
                            waits=[wd_done, hg_tok, first_sc], inc=(nc4 == 3))
                for nc4 in range(4):
                    nc_i = ncg * 4 + nc4
                    zt_ = qB.dma(lambda e, nc_i=nc_i, nc4=nc4: e.dma_start(
                        out=nzd[nc4 % 2], in_=zsd_d[:, nc_i * P:(nc_i + 1) * P]),
                        waits=[S.tok("tensor")])
                    S.op("tensor", lambda e, nc4=nc4: e.matmul(
                        PSA[:, nc4 * 512:(nc4 + 1) * 512],
                        nzd[nc4 % 2][:], hg_sb[:],
                        start=False, stop=True, skip_group_check=True),
                        waits=[zt_, hg_tok], inc=True)
                    mt_ = S.tok("tensor")
                    stg = ff_buf[nc_i]
                    ev = S.op("scalar", lambda e, nc4=nc4, stg=stg: e.activation(
                        out=stg, in_=PSA[:, nc4 * 512:(nc4 + 1) * 512],
                        func=AF.Copy), waits=[mt_], inc=True)
                    cp = S.op("scalar", lambda e, stg=stg: e.activation(
                        out=st_cp, in_=stg, func=AF.Copy), waits=[ev], inc=True)
                    sq = S.op("scalar", lambda e, stg=stg: e.activation(
                        out=st_sq, in_=stg, func=AF.Square), waits=[cp], inc=True)
                    S.op("tensor", lambda e, nc_i=nc_i: e.matmul(
                        PSA[0:1, 3072:3584], onescol_b[:], st_cp[:],
                        start=(nc_i == 0), stop=(nc_i == KC - 1),
                        skip_group_check=True), waits=[cp])
                    S.op("tensor", lambda e, nc_i=nc_i: e.matmul(
                        PSA[0:1, 3584:4096], onescol_b[:], st_sq[:],
                        start=(nc_i == 0), stop=(nc_i == KC - 1),
                        skip_group_check=True), waits=[sq], inc=True)
                    ffn_done.append(sq)

            if stop == "ffn_sp" + sfx:
                emit_export("hg", hg_sb[:], FC, MT, True, [ffn_done[-1]],
                            tmp_off=58368)
                for nc_i in range(KC):
                    qY.dma(lambda e, nc_i=nc_i: e.dma_start(
                        out=at_sp[nc_i * P:(nc_i + 1) * P, :], in_=ff_buf[nc_i]),
                        waits=[ffn_done[-1], S.tok("tensor")])
                S.emit("sync", lambda e: e.nop(), waits=qY.final_waits())
                stopped = True
                break

            bc4 = ln_row_scalars(PSA[0:1, 3072:3584], PSA[0:1, 3584:4096],
                                 BK(4), BK(5),
                                 [S.tok("tensor"), ffn_done[-1]])
            out_toks = []
            y_rts = []
            for nc_i in range(KC):
                ow = [bc4] + ([out_toks[nc_i - 2]] if nc_i >= 2 else [])
                y_rts.append(qY.dma(lambda e, nc_i=nc_i, t=t: e.dma_start(
                    out=y1_ld[nc_i % 2], in_=y1_sp[t, nc_i * P:(nc_i + 1) * P, :]),
                    waits=[bc4] + ([S.tok("vector")] if nc_i >= 2 else [])))
                v1 = S.op("vector", lambda e, nc_i=nc_i: e.tensor_tensor(
                    out=y2_st[nc_i % 2], in0=ff_buf[nc_i], in1=PSA[:, 2048:2560],
                    op=ALU.mult), waits=ow, inc=True)
                v1 = S.op("vector", lambda e, nc_i=nc_i: e.tensor_tensor(
                    out=y2_st[nc_i % 2], in0=y2_st[nc_i % 2], in1=PSA[:, 2560:3072],
                    op=ALU.add), waits=[v1], inc=True)
                v1 = S.op("vector", lambda e, nc_i=nc_i: e.tensor_tensor(
                    out=y2_st[nc_i % 2], in0=y2_st[nc_i % 2], in1=y1_ld[nc_i % 2],
                    op=ALU.add), waits=[v1, y_rts[nc_i]], inc=True)
                out_toks.append(qX.dma(
                    lambda e, nc_i=nc_i, tok0=tok0: e.dma_start(
                        out=out_d[nc_i * P:(nc_i + 1) * P, tok0:tok0 + MT],
                        in_=y2_st[nc_i % 2]), waits=[v1]))
            if stop == "tile0" + sfx:
                stopped = True
                break

        fw = qX.final_waits() + qY.final_waits() + qB.final_waits() + qW.final_waits()
        S.emit("sync", lambda e: e.nop(), waits=fw)

        block = st.enter_context(nc.Block())

        def mk_runner(eng):
            items = S.q[eng]

            def run(e):
                for fn, w, ii in items:
                    for sh, v in w:
                        e.wait_ge(sh, v)
                    ins = fn(e)
                    for sh, k in ii:
                        ins.then_inc(sh, k)
            return run

        for eng in ENGINES:
            getattr(block, eng)(mk_runner(eng))
    return nc, dbg



def _prep_core_inputs(core, inp):
    x = np.ascontiguousarray(inp["L0_attn_inputs"].reshape(NCORES * TPC, D))
    g0 = core * TPC
    seq = g0 // SEQ
    s0, s1 = seq * SEQ, (seq + 1) * SEQ
    xc = np.zeros((TPC + 2 * P, D), np.float32)
    lo, hi = max(g0 - P, s0), min(g0 + TPC + P, s1)
    xc[lo - (g0 - P): hi - (g0 - P)] = x[lo:hi]

    u8 = lambda a: np.ascontiguousarray(a).view(np.uint8)
    gb = u8(inp["L0_ffn_gate_qweight"]).reshape(KC, P, FC, 64)
    gb = np.ascontiguousarray(gb.transpose(2, 1, 0, 3).reshape(FC, P, KC * 64))
    ub = u8(inp["L0_ffn_up_qweight"]).reshape(KC, P, FC, 64)
    ub = np.ascontiguousarray(ub.transpose(2, 1, 0, 3).reshape(FC, P, KC * 64))
    dbv = u8(inp["L0_ffn_down_qweight"]).reshape(FC, P, 8, 256)
    dbv = np.ascontiguousarray(dbv.transpose(2, 0, 1, 3))

    m = dict(
        x_in=xc,
        convw=np.ascontiguousarray(inp["L0_attn_conv_w"]),
        convb=np.ascontiguousarray(inp["L0_attn_conv_b"].reshape(CONVO, 1)),
        wq_in=np.ascontiguousarray(inp["L0_attn_wq"].reshape(W_, W_)),
        wk_in=np.ascontiguousarray(inp["L0_attn_wk"].reshape(W_, W_)),
        wv_in=np.ascontiguousarray(inp["L0_attn_wv"].reshape(W_, W_)),
        wo_in=np.ascontiguousarray(inp["L0_attn_wo"].reshape(W_, W_)),
        bq_in=np.ascontiguousarray(inp["L0_attn_bq"].reshape(W_, 1)),
        bk_in=np.ascontiguousarray(inp["L0_attn_bk"].reshape(W_, 1)),
        bv_in=np.ascontiguousarray(inp["L0_attn_bv"].reshape(1, W_)),
        bo_in=np.ascontiguousarray(inp["L0_attn_bo"].reshape(W_, 1)),
        projw=np.ascontiguousarray(inp["L0_attn_proj_w"]),
        projb=np.ascontiguousarray(inp["L0_attn_proj_b"].reshape(KC, P).T),
        gb_in=gb, ub_in=ub, db_in=dbv,
        gs_in=np.ascontiguousarray(inp["L0_ffn_gate_scales"]),
        us_in=np.ascontiguousarray(inp["L0_ffn_up_scales"]),
        ds_in=np.ascontiguousarray(inp["L0_ffn_down_scales"]),
        gz_in=u8(inp["L0_ffn_gate_qzeros"]).reshape(KC, DFF // 2),
        uz_in=u8(inp["L0_ffn_up_qzeros"]).reshape(KC, DFF // 2),
        dz_in=u8(inp["L0_ffn_down_qzeros"]).reshape(FC, D // 2),
    )
    return m


def kernel(**inputs):
    nc, _ = build_core()
    in_maps = [_prep_core_inputs(c, inputs) for c in range(NCORES)]
    res = run_bass_kernel_spmd(nc, in_maps, core_ids=list(range(NCORES)))
    outs = [np.ascontiguousarray(r["out_d"].T) for r in res.results]
    return np.concatenate(outs, axis=0).reshape(4, SEQ, D)
